# version 17
# baseline (speedup 1.0000x reference)
import contextlib

import numpy as np

import concourse.bass as bass
import concourse.mybir as mybir
from concourse.bass_utils import run_bass_kernel_spmd

F32 = mybir.dt.float32
BF16 = mybir.dt.bfloat16
AF = mybir.ActivationFunctionType
ALU = mybir.AluOpType
AX = mybir.AxisListType

L = 2048
D = 2048
DI = 4096
DS = 16
H = 8192
TCH = 256
NCH = L // TCH
NT = 16
KD = D // 128
KDI = DI // 128
KH = H // 128
NLD = 17
NST = 8

_CACHE = {}
CUT = None


def build_program():
    nc = bass.Bass("TRN2", target_bir_lowering=False)

    xrow = nc.dram_tensor("xrow", [L, D], F32, kind="ExternalInput").ap()
    xcol = nc.dram_tensor("xcol", [D, L], F32, kind="ExternalInput").ap()
    winT = nc.dram_tensor("winT", [D, 2 * DI], F32, kind="ExternalInput").ap()
    wxT = nc.dram_tensor("wxT", [128, KDI, 160], F32, kind="ExternalInput").ap()
    wdtT = nc.dram_tensor("wdtT", [128, DI], F32, kind="ExternalInput").ap()
    woT = nc.dram_tensor("woT", [DI, D], F32, kind="ExternalInput").ap()
    wgT = nc.dram_tensor("wgT", [D, H], F32, kind="ExternalInput").ap()
    wuT = nc.dram_tensor("wuT", [D, H], F32, kind="ExternalInput").ap()
    wdT = nc.dram_tensor("wdT", [H, D], F32, kind="ExternalInput").ap()
    convw = nc.dram_tensor("convw", [128, KDI, 4], F32, kind="ExternalInput").ap()
    convb = nc.dram_tensor("convb", [128, KDI], F32, kind="ExternalInput").ap()
    dtb = nc.dram_tensor("dtb", [128, KDI], F32, kind="ExternalInput").ap()
    alog = nc.dram_tensor("alog", [128, KDI, DS], F32, kind="ExternalInput").ap()
    dvec = nc.dram_tensor("dvec", [128, KDI], F32, kind="ExternalInput").ap()

    outT = nc.dram_tensor("outT", [128, KD, L], F32, kind="ExternalOutput").ap()
    wgusp = nc.dram_tensor("wgusp", [128, KH, 4096], BF16, kind="Internal").ap()
    xsp = nc.dram_tensor("xsp", [128, KDI, L], BF16, kind="Internal").ap()
    zsp = nc.dram_tensor("zsp", [128, KDI, L], BF16, kind="Internal").ap()
    qsp = nc.dram_tensor("qsp", [128, KD, L], BF16, kind="Internal").ap()
    ygsp = nc.dram_tensor("ygsp", [128, 4, KDI, 512], BF16, kind="Internal").ap()
    psp = nc.dram_tensor("psp", [128, 4, KH, 512], BF16, kind="Internal").ap()
    offn = nc.dram_tensor("offn", [128, KD, L], F32, kind="Internal").ap()

    streams = {k: [] for k in ("sync", "tensor", "vector", "scalar", "gpsimd")}
    C = {"ve": 0, "sc": 0, "te": 0, "gp": 0}
    ld_cnt = [0] * NLD
    st_cnt = [0] * NST
    ld_rr = [0]
    st_rr = [0]

    with contextlib.ExitStack() as ctx:
        sems = {name: ctx.enter_context(nc.semaphore(name)) for name in C}
        ldsem = [ctx.enter_context(nc.semaphore(f"la{i}")) for i in range(NLD)]
        stsem = [ctx.enter_context(nc.semaphore(f"sa{i}")) for i in range(NST)]

        def W(eng, sem, val):
            if val and val > 0:
                streams[eng].append(lambda e, s=sems[sem], v=val: e.wait_ge(s, v))

        _SELF = {"vector": "ve", "scalar": "sc", "gpsimd": "gp"}

        def I(eng, builder, sem=None, amt=1, nodep=False):
            own = _SELF.get(eng)
            if nodep:
                own = None
            if own is not None and C[own] > 0 and sem is not None:
                streams[eng].append(
                    lambda e, s=sems[own], v=C[own]: e.wait_ge(s, v))
            if sem is not None:
                C[sem] += amt
                streams[eng].append(
                    lambda e, b=builder, s=sems[sem], a=amt: b(e).then_inc(s, a))
            else:
                streams[eng].append(lambda e, b=builder: b(e))
            return C[sem] if sem else None

        def LD(builder, lane):
            i = lane
            ld_cnt[i] += 16
            streams["sync"].append(
                lambda e, b=builder, s=ldsem[i]: b(e).then_inc(s, 16))
            return ("L", i, ld_cnt[i])

        def ST(builder, lane):
            i = lane
            st_cnt[i] += 16
            streams["gpsimd"].append(
                lambda e, b=builder, s=stsem[i]: b(e).then_inc(s, 16))
            return ("S", i, st_cnt[i])

        def LDG(builder, lane):
            i = lane
            ld_cnt[i] += 16
            streams["gpsimd"].append(
                lambda e, b=builder, s=ldsem[i]: b(e).then_inc(s, 16))
            return ("L", i, ld_cnt[i])

        def WT(eng, *toks):
            for tok in toks:
                if tok is None:
                    continue
                kind, i, v = tok
                s = ldsem[i] if kind == "L" else stsem[i]
                streams[eng].append(lambda e, s=s, v=v: e.wait_ge(s, v))

        def st_snap():
            return [("S", i, st_cnt[i]) for i in range(NST) if st_cnt[i]]

        PA = ctx.enter_context(nc.sbuf_tensor("PA", [128, 40960], F32))
        PAb = PA[:].bitcast(BF16)

        oneh = ctx.enter_context(nc.sbuf_tensor("oneh", [32, 32 * 128], BF16))
        wx_bf = ctx.enter_context(nc.sbuf_tensor("wx_bf", [128, KDI, 160], BF16))
        wdt_bf = ctx.enter_context(nc.sbuf_tensor("wdt_bf", [128, DI], BF16))
        a_sb = ctx.enter_context(nc.sbuf_tensor("a_sb", [128, KDI, DS], F32))
        convw_sb = ctx.enter_context(nc.sbuf_tensor("convw_sb", [128, KDI, 4], F32))
        convb_sb = ctx.enter_context(nc.sbuf_tensor("convb_sb", [128, KDI], F32))
        dtb_sb = ctx.enter_context(nc.sbuf_tensor("dtb_sb", [128, KDI], F32))
        d_sb = ctx.enter_context(nc.sbuf_tensor("d_sb", [128, KDI], F32))
        hcar = ctx.enter_context(nc.sbuf_tensor("hcar", [128, KDI, DS], F32))
        dtlo = ctx.enter_context(nc.sbuf_tensor("dtlo", [128, TCH], BF16))
        bc_sb = ctx.enter_context(nc.sbuf_tensor("bc_sb", [32, TCH], BF16))
        dAs_t = [ctx.enter_context(nc.sbuf_tensor(f"dA{i}", [128, TCH], F32))
                 for i in range(15)]
        stat = ctx.enter_context(nc.sbuf_tensor("stat", [128, 8 * NT], F32))
        identb = ctx.enter_context(nc.sbuf_tensor("identb", [128, 128], BF16))
        cst = ctx.enter_context(nc.sbuf_tensor("cst", [128, 2], F32))
        eps5 = cst[:, 0:1]
        eps8 = cst[:, 1:2]

        pb = [ctx.enter_context(nc.psum_tensor(f"pb{i}", [128, 512], F32))
              for i in range(8)]

        block = ctx.enter_context(nc.Block())

        def vf(off, *shape):
            n = int(np.prod(shape))
            ap = PA[:, off:off + n]
            if len(shape) == 2:
                ap = ap.rearrange("p (a b) -> p a b", b=shape[-1])
            elif len(shape) == 3:
                ap = ap.rearrange("p (a b c) -> p a b c", b=shape[1], c=shape[2])
            return ap

        def vb(off_f32, *shape):
            n = int(np.prod(shape))
            ap = PAb[:, 2 * off_f32: 2 * off_f32 + n]
            if len(shape) == 3:
                ap = ap.rearrange("p (a b c) -> p a b c", b=shape[1], c=shape[2])
            elif len(shape) == 2:
                ap = ap.rearrange("p (a b) -> p a b", b=shape[-1])
            return ap

        wx_stage = PA[:, 0:KDI * 160]
        wdt_stage = PA[:, 5120:5120 + DI]
        t_wx = LD(lambda e: e.dma_start(out=wx_stage, in_=wxT.rearrange("p a b -> p (a b)")), 0)
        t_wdt = LD(lambda e: e.dma_start(out=wdt_stage, in_=wdtT), 0)
        t_a = LD(lambda e: e.dma_start(out=a_sb[:], in_=alog), 0)
        t_cw = LD(lambda e: e.dma_start(out=convw_sb[:], in_=convw), 0)
        t_cb = LD(lambda e: e.dma_start(out=convb_sb[:], in_=convb), 0)
        t_db = LD(lambda e: e.dma_start(out=dtb_sb[:], in_=dtb), 0)
        t_dv = LD(lambda e: e.dma_start(out=d_sb[:], in_=dvec), 0)

        WT("vector", t_dv)
        I("vector", lambda e: e.tensor_copy(
            wx_bf[:].rearrange("p a b -> p (a b)"), wx_stage), "ve")
        I("vector", lambda e: e.tensor_copy(wdt_bf[:], wdt_stage), "ve")
        WT("scalar", t_dv)
        sc_aexp = I("scalar", lambda e: e.activation(
            a_sb[:].rearrange("p a b -> p (a b)"),
            a_sb[:].rearrange("p a b -> p (a b)"), AF.Exp), "sc")
        W("vector", "sc", sc_aexp)
        I("vector", lambda e: e.tensor_scalar(
            a_sb[:].rearrange("p a b -> p (a b)"),
            a_sb[:].rearrange("p a b -> p (a b)"), -1.0, None, ALU.mult), "ve")
        I("vector", lambda e: e.memset(cst[:, 0:1], 1e-5), "ve")
        I("vector", lambda e: e.memset(cst[:, 1:2], 1e-8), "ve")
        tmpI = PA[:, 16384:16384 + 128]
        tmpO = PA[0:32, 16512:16512 + 4096]
        gp_o = None
        I("gpsimd", lambda e: e.iota(tmpI, [[-1, 128]], base=0, channel_multiplier=1,
                                     allow_small_or_imprecise_dtypes=True), "gp")
        gp_o = I("gpsimd", lambda e: e.iota(tmpO, [[-1, 32], [0, 128]], base=0,
                                            channel_multiplier=1,
                                            allow_small_or_imprecise_dtypes=True), "gp")
        W("vector", "gp", gp_o)
        I("vector", lambda e: e.tensor_single_scalar(identb[:], tmpI, 0.0, ALU.is_equal), "ve")
        I("vector", lambda e: e.tensor_single_scalar(oneh[:], tmpO, 0.0, ALU.is_equal), "ve")
        I("vector", lambda e: e.memset(hcar[:].rearrange("p a b -> p (a b)"), 0.0), "ve")
        ve_p0 = C["ve"]
        MARKS = {}
        MARKS["P0"] = {k: len(v) for k, v in streams.items()}

        xr = [vf(0, 2048), vf(2048, 2048)]
        t0 = vf(8192, 2048)
        tq = vf(10240, 2048)
        mk = vf(12288, 2048)
        xntok = vb(14336, 2048)
        xqtok = vb(15360, 2048)
        qtt = vb(16384, NT, 128)
        xnT = vb(18432, KD, 2048)
        bnb = vf(4096, NT, 24)
        magg = vf(4480, NT, 2)
        stat2 = vf(4512, 3, NT)
        xrow_t = xrow.rearrange("(t p) d -> p t d", p=128)

        def scol(j):
            return stat[:, j * NT:(j + 1) * NT]

        ev_xrA = [0, 0]
        for tt in range(NT):
            sl = tt % 2
            W("sync", "ve", max(ev_xrA[sl], ve_p0 if tt == 0 else 0))
            t_xr = LD(lambda e, tt=tt, sl=sl: e.dma_start(out=xr[sl], in_=xrow_t[:, tt, :]), 1 + sl)
            WT("vector", t_xr)
            for bq in range(4):
                I("vector", lambda e, tt=tt, sl=sl, bq=bq: e.bn_stats(
                    bnb[:, tt, bq * 6:(bq + 1) * 6],
                    xr[sl].rearrange("p (a b) -> p a b", b=512)[:, bq, :]), "ve")
            I("vector", lambda e, tt=tt: e.bn_aggr(magg[:, tt, :], bnb[:, tt, :]), "ve")
            I("vector", lambda e, tt=tt, sl=sl: e.reduce_max(
                stat2[:, 0, tt:tt + 1], xr[sl], AX.X), "ve")
            ev_xrA[sl] = I("vector", lambda e, tt=tt, sl=sl: e.tensor_reduce(
                stat2[:, 1, tt:tt + 1], xr[sl], AX.X, ALU.min), "ve")

        m_s = magg[:, :, 0]
        v_s = magg[:, :, 1]
        r_s, r2_s, rs_s, c2_s, nmr_s, b2_s = (scol(2), scol(3), scol(4),
                                              scol(5), scol(6), scol(7))
        mx_s, mn_s, rs128_s = stat2[:, 0, :], stat2[:, 1, :], stat2[:, 2, :]
        t6_s, t7_s = scol(0), scol(1)
        ve_A = C["ve"]
        W("scalar", "ve", ve_A)
        I("scalar", lambda e: e.activation(t6_s, v_s, AF.Ln, bias=eps5), "sc")
        sc_r = I("scalar", lambda e: e.activation(r_s, t6_s, AF.Exp, scale=-0.5), "sc")
        W("vector", "sc", sc_r)
        I("vector", lambda e: e.tensor_tensor(t6_s, v_s, r_s, ALU.mult), "ve")
        ve_v2 = I("vector", lambda e: e.tensor_tensor(t6_s, t6_s, r_s, ALU.mult), "ve")
        W("scalar", "ve", ve_v2)
        I("scalar", lambda e: e.activation(t6_s, t6_s, AF.Ln, bias=eps8), "sc")
        sc_r2 = I("scalar", lambda e: e.activation(r2_s, t6_s, AF.Exp, scale=-0.5), "sc")
        W("vector", "sc", sc_r2)
        I("vector", lambda e: e.tensor_tensor(t6_s, mx_s, m_s, ALU.subtract), "ve")
        I("vector", lambda e: e.tensor_tensor(t7_s, m_s, mn_s, ALU.subtract), "ve")
        I("vector", lambda e: e.tensor_tensor(t6_s, t6_s, t7_s, ALU.max), "ve")
        I("vector", lambda e: e.tensor_tensor(t6_s, t6_s, r_s, ALU.mult), "ve")
        I("vector", lambda e: e.tensor_tensor(t6_s, t6_s, r2_s, ALU.mult), "ve")
        I("vector", lambda e: e.reciprocal(t7_s, t6_s), "ve")
        I("vector", lambda e: e.tensor_scalar(rs_s, t6_s, 1.0 / 127.0, None, ALU.mult), "ve")
        I("vector", lambda e: e.tensor_scalar(rs128_s, rs_s, 128.0, None, ALU.mult), "ve")
        I("vector", lambda e: e.tensor_scalar(t7_s, t7_s, 127.0, None, ALU.mult), "ve")
        I("vector", lambda e: e.tensor_tensor(c2_s, r_s, r2_s, ALU.mult), "ve")
        I("vector", lambda e: e.tensor_tensor(c2_s, c2_s, t7_s, ALU.mult), "ve")
        I("vector", lambda e: e.tensor_tensor(nmr_s, m_s, r_s, ALU.mult), "ve")
        I("vector", lambda e: e.tensor_scalar(nmr_s, nmr_s, -1.0, None, ALU.mult), "ve")
        I("vector", lambda e: e.tensor_tensor(b2_s, m_s, c2_s, ALU.mult), "ve")
        ve_stats = I("vector", lambda e: e.tensor_scalar(
            b2_s, b2_s, -1.0, 128.5, ALU.mult, ALU.add), "ve")

        ev_xr_sc = [0, 0]
        ev_tq = [0] * 8
        ev_tok_te = 0
        t_qtt = None
        for tt in range(NT):
            sl = tt % 2
            W("sync", "sc", ev_xr_sc[sl])
            W("sync", "ve", ve_stats if tt < 2 else 0)
            t_xr = LD(lambda e, tt=tt, sl=sl: e.dma_start(out=xr[sl], in_=xrow_t[:, tt, :]), 1 + sl)
            WT("scalar", t_xr)
            if tt == 0:
                W("scalar", "ve", ve_stats)
            W("scalar", "te", ev_tok_te)
            I("scalar", lambda e, sl=sl, tt=tt: e.activation(
                xntok, xr[sl], AF.Identity, scale=r_s[:, tt:tt + 1],
                bias=nmr_s[:, tt:tt + 1]), "sc")
            sc_t0 = I("scalar", lambda e, sl=sl, tt=tt: e.activation(
                t0, xr[sl], AF.Identity, scale=c2_s[:, tt:tt + 1],
                bias=b2_s[:, tt:tt + 1]), "sc")
            ev_xr_sc[sl] = sc_t0
            W("vector", "sc", sc_t0)
            I("vector", lambda e: e.tensor_copy(tq, t0), "ve")
            for cbit in (128.0, 64.0, 32.0, 16.0, 8.0, 4.0, 2.0, 1.0):
                I("vector", lambda e, c=cbit: e.tensor_single_scalar(mk, tq, c, ALU.is_ge), "ve")
                I("vector", lambda e, c=cbit: e.scalar_tensor_tensor(
                    tq, mk, -c, tq, ALU.mult, ALU.add), "ve")
            I("vector", lambda e: e.tensor_tensor(t0, t0, tq, ALU.subtract), "ve")
            ve_xq = I("vector", lambda e, tt=tt: e.tensor_scalar(
                xqtok, t0, rs_s[:, tt:tt + 1], rs128_s[:, tt:tt + 1],
                ALU.mult, ALU.subtract), "ve")
            for j in range(2 * KD):
                qn = j % 8
                bank_b = pb[qn][:].bitcast(BF16)
                quarter = bank_b[:, 0:128]
                W("tensor", "ve", ve_xq)
                W("tensor", "sc", ev_tq[qn])
                src_ = xntok if j < KD else xqtok
                dcol = j % KD
                te_t = I("tensor", lambda e, q=quarter, s=src_, dcol=dcol: e.transpose(
                    q, s.rearrange("p (a b) -> p a b", b=128)[:, dcol, :], identb[:]), "te")
                W("scalar", "te", te_t)
                if j == KD and tt > 0:
                    WT("scalar", t_qtt)
                if j < KD:
                    ev_tq[qn] = I("scalar", lambda e, q=quarter, dcol=dcol, tt=tt: e.activation(
                        xnT[:, dcol, tt * 128:(tt + 1) * 128], q, AF.Copy), "sc")
                else:
                    ev_tq[qn] = I("scalar", lambda e, q=quarter, dcol=dcol: e.activation(
                        qtt[:, dcol, :], q, AF.Copy), "sc")
            ev_tok_te = C["te"]
            sc_tile = C["sc"]
            W("gpsimd", "sc", sc_tile)
            for h2 in range(2):
                t_qtt = ST(lambda e, tt=tt, h2=h2: e.dma_start(
                    out=qsp[:, h2 * 8:(h2 + 1) * 8, tt * 128:(tt + 1) * 128],
                    in_=qtt[:, h2 * 8:(h2 + 1) * 8, :]), 0)
        ve_p1 = C["ve"]
        sc_p1 = C["sc"]
        snap_qsp = st_snap()

        wstage2 = [vf(0, 16, 512), vf(8192, 16, 512)]
        wbf = vb(34816, 16, 512)
        xzst = vb(38912, 2, 2048)
        winT_t = winT.rearrange("(k p) (g c) -> p k g c", p=128, c=512)

        wgT_t = wgT.rearrange("(k p) (h c) -> p k h c", p=128, c=128)
        wuT_t = wuT.rearrange("(k p) (h c) -> p k h c", p=128, c=128)
        cslot_f = [vf(16384 + 512 * i, 4, 128) for i in range(4)]
        cslot_b = [vb(16384 + 512 * i, 512) for i in range(4)]
        cst_tok = [None] * 512
        cld_tok = [None] * 512
        wgu_tok = [None] * KH
        csc = [0, 0, 0, 0]

        def cast_piece_load(p):
            ht, wu, h = p // 8, (p // 4) % 2, p % 4
            sl = p % 4
            if p >= 4:
                WT("sync", cst_tok[p - 4])
            src = wgT_t if wu == 0 else wuT_t
            cld_tok[p] = LD(lambda e, ht=ht, h=h, src=src, sl=sl: e.dma_start(
                out=cslot_f[sl][:], in_=src[:, h * 4:(h + 1) * 4, ht, :]), 8 + sl)

        def cast_piece_do(p):
            ht, wu, h = p // 8, (p // 4) % 2, p % 4
            sl = p % 4
            WT("scalar", cld_tok[p])
            csc[sl] = I("scalar", lambda e, sl=sl: e.activation(
                cslot_b[sl][:], cslot_f[sl][:].rearrange("p a b -> p (a b)"),
                AF.Sign), "sc")
            W("gpsimd", "sc", csc[sl])
            if p >= 2:
                WT("gpsimd", cst_tok[p - 2])
            cst_tok[p] = ST(lambda e, ht=ht, wu=wu, h=h, sl=sl: e.dma_start(
                out=wgusp[:, ht, (wu * 4 + h) * 512:(wu * 4 + h + 1) * 512],
                in_=cslot_b[sl][:]), 3 + p % 2)
            if p % 8 == 7:
                wgu_tok[ht] = (cst_tok[p - 1], cst_tok[p])
        ev_wst_ve = [ve_p1, ve_p1]
        ev_wbf_te = 0
        ev_bank_sc = [ve_p1] * 4
        W("tensor", "sc", sc_p1)
        t_row = [None, None]
        tok_w = [None, None]
        W("sync", "ve", ve_p1)
        W("sync", "sc", sc_p1)
        for mg in range(2):
            wsl = mg % 2
            t_w0 = LD(lambda e, mg=mg, wsl=wsl: e.dma_start(out=wstage2[wsl][:, 0:8, :], in_=winT_t[:, 0:8, mg, :]), 3 + 10 * wsl)
            t_w1 = LD(lambda e, mg=mg, wsl=wsl: e.dma_start(out=wstage2[wsl][:, 8:16, :], in_=winT_t[:, 8:16, mg, :]), 4 + 10 * wsl)
            tok_w[wsl] = (t_w0, t_w1)
        for tok in snap_qsp:
            WT("sync", tok)
        for p0 in range(4):
            cast_piece_load(p0)
        for mg in range(16):
            wsl = mg % 2
            WT("vector", *tok_w[wsl])
            W("vector", "te", ev_wbf_te)
            if mg == 0:
                for tok in snap_qsp:
                    WT("vector", tok)
            ev_cast = I("vector", lambda e, wsl=wsl: e.tensor_copy(
                wbf[:].rearrange("p a b -> p (a b)"), wstage2[wsl][:].rearrange("p a b -> p (a b)")), "ve")
            ev_wst_ve[wsl] = ev_cast
            if mg + 2 < 16:
                W("sync", "ve", ev_cast)
                t_w0 = LD(lambda e, mg=mg, wsl=wsl: e.dma_start(out=wstage2[wsl][:, 0:8, :], in_=winT_t[:, 0:8, mg + 2, :]), 3 + 10 * wsl)
                t_w1 = LD(lambda e, mg=mg, wsl=wsl: e.dma_start(out=wstage2[wsl][:, 8:16, :], in_=winT_t[:, 8:16, mg + 2, :]), 4 + 10 * wsl)
                tok_w[wsl] = (t_w0, t_w1)
            for ms in range(4):
                row = mg * 4 + ms
                xsl = ms % 2
                for wdw in range(4):
                    bank = (ms * 4 + wdw) % 4
                    W("tensor", "ve", ev_cast)
                    W("tensor", "ve", ev_bank_sc[bank])
                    for k in range(16):
                        I("tensor", lambda e, k=k, ms=ms, w=wdw, bank=bank: e.matmul(
                            pb[bank][:], wbf[:, k, ms * 128:(ms + 1) * 128],
                            xnT[:, k, w * 512:(w + 1) * 512],
                            start=(k == 0), stop=(k == 15)), "te")
                    te_mm = C["te"]
                    W("vector", "te", te_mm)
                    if wdw == 0:
                        WT("vector", t_row[xsl])
                    ev_bank_sc[bank] = I("vector", lambda e, xsl=xsl, w=wdw, bank=bank: e.tensor_copy(
                        xzst[:, xsl, w * 512:(w + 1) * 512], pb[bank][:]), "ve")
                sc_row = C["ve"]
                W("gpsimd", "ve", sc_row)
                dst = xsp if row < KDI else zsp
                drow = row if row < KDI else row - KDI
                t_row[xsl] = ST(lambda e, dst=dst, drow=drow, xsl=xsl: e.dma_start(
                    out=dst[:, drow, :], in_=xzst[:, xsl, :]), 1 + xsl)
                for p in range(8 * row, 8 * row + 8):
                    cast_piece_do(p)
                    if p + 4 < 512:
                        cast_piece_load(p + 4)
            ev_wbf_te = C["te"]
        snap_p2 = st_snap()
        te_p2 = C["te"]
        ve_p2c = C["ve"]
        sc_p2c = C["sc"]
        MARKS["P2"] = {k: len(v) for k, v in streams.items()}

        xc_ = vf(0, KDI, TCH)
        xcf = vf(0, KDI * TCH)
        delta = vb(8192, KDI, TCH)
        deltaf = vb(8192, KDI * TCH)
        wgu_sl = [vb(12288, 2, KD, 128), vb(30640, 2, KD, 128)]
        cq_t = vf(14336, 8, TCH)
        spq_t = vf(14336, 2048)
        xpre = vb(16384, KDI, 259)
        hall = vf(16384, DS, TCH)
        hallf = vf(16384, DS * TCH)
        ct = vb(20528, DS, TCH)
        qwin = vb(22576, KD, TCH)
        zb = vb(24624, KDI, TCH)
        yg8 = vb(28720, 8, TCH)
        dxdtb = vb(29744, TCH)
        pst2 = vb(29872, TCH)
        ydtb = vb(30000, TCH)
        sgb2 = [vb(30128, TCH), vb(30512, TCH)]
        xcb = vb(32768, KDI, TCH)
        xcbf = vb(32768, KDI * TCH)
        dbx_b = vb(32768, DS, TCH)
        b_all = vb(36864, DS, TCH)
        c_all = vb(38912, DS, TCH)

        dAs_t = dAs_t + [vf(30256, TCH)]
        ev_xpre_ve = ve_p1
        ev_xpre_gp = 0
        ev_zread_ve = 0
        te_xp = 0
        t_yg8 = [None, None, None, None]
        bank_sc = [sc_p2c] * 4
        bank_ve = [ve_p2c] * 4
        unit_te = {}
        silu_sc = {}
        wgu_te = [0, 0]
        wgu_ld_tok = [None, None]
        qwin_te = [te_p2]
        t_qw = [None, None]
        pst_tok = [None, None]
        sgb_ve = [0, 0]
        exp_sc = [0] * 16

        def wgu_load(ht, cI):
            sl = ht % 2
            W("sync", "te", wgu_te[sl])
            if cI == 0 and ht < 2:
                W("sync", "ve", ve_p2c)
                W("sync", "te", te_p2)
                WT("sync", *wgu_tok[ht])
            wgu_ld_tok[sl] = LD(lambda e, ht=ht, sl=sl: e.dma_start(
                out=wgu_sl[sl][:].rearrange("p a k c -> p (a k c)"),
                in_=wgusp[:, ht, :]), 15 + sl)

        def unit_mm(u):
            ht = u % KH
            sl = ht % 2
            bank = pb[u % 4]
            WT("tensor", t_qw[1])
            WT("tensor", wgu_ld_tok[sl])
            W("tensor", "sc", bank_sc[u % 4])
            W("tensor", "ve", bank_ve[u % 4])
            for k in range(KD):
                I("tensor", lambda e, k=k, bank=bank, sl=sl: e.matmul(
                    bank[:, 0:TCH], wgu_sl[sl][:, 0, k, :], qwin[:, k, :],
                    start=(k == 0), stop=(k == KD - 1)), "te")
            for k in range(KD):
                I("tensor", lambda e, k=k, bank=bank, sl=sl: e.matmul(
                    bank[:, TCH:2 * TCH], wgu_sl[sl][:, 1, k, :], qwin[:, k, :],
                    start=(k == 0), stop=(k == KD - 1)), "te")
            unit_te[u] = C["te"]
            wgu_te[sl] = C["te"]
            qwin_te[0] = C["te"]
            if ht + 2 < KH:
                wgu_load(ht + 2, u // KH)

        def unit_silu(u):
            bank = pb[u % 4]
            W("scalar", "te", unit_te[u])
            W("scalar", "ve", sgb_ve[u % 2])
            silu_sc[u] = I("scalar", lambda e, bank=bank, sl2=u % 2: e.activation(
                sgb2[sl2][:], bank[:, 0:TCH], AF.Silu), "sc")

        def unit_pst(u):
            ht = u % KH
            ucol = (u // KH) * TCH
            sl2 = u % 2
            W("vector", "sc", silu_sc[u])
            WT("vector", pst_tok[0])
            pv = I("vector", lambda e, bank=pb[u % 4], sl2=sl2: e.tensor_tensor(
                pst2[:], sgb2[sl2][:], bank[:, TCH:2 * TCH], ALU.mult), "ve")
            bank_ve[u % 4] = pv
            bank_sc[u % 4] = silu_sc[u]
            sgb_ve[u % 2] = pv
            W("gpsimd", "ve", pv)
            pst_tok[0] = ST(lambda e, ht=ht, w4=ucol // 512, coff=ucol % 512,
                            sl2=sl2: e.dma_start(
                out=psp[:, w4, ht, coff:coff + TCH], in_=pst2[:]), 6 + sl2)

        for cI in range(NCH):
            col0 = cI * TCH
            w4 = col0 // 512
            coff = col0 % 512
            W("sync", "ve", ev_xpre_ve)
            W("sync", "gp", ev_xpre_gp)
            W("sync", "te", qwin_te[0])
            if cI == 0:
                for tok in snap_qsp:
                    WT("sync", tok)
            t_qw = [LD(lambda e, col0=col0, h2=h2: e.dma_start(
                out=qwin[:, h2 * 8:(h2 + 1) * 8, :],
                in_=qsp[:, h2 * 8:(h2 + 1) * 8, col0:col0 + TCH]), 7)
                for h2 in range(2)]
            wgu_load(0, cI)
            wgu_load(1, cI)
            if cI == 0:
                for tok in snap_p2:
                    WT("sync", tok)
                W("vector", "te", te_p2)
                W("vector", "sc", sc_p2c)
                for tok in snap_qsp:
                    WT("vector", tok)
                WT("vector", *wgu_tok[KH - 1])
                ve_z0 = I("vector", lambda e: e.memset(
                    xpre[:].rearrange("p a b -> p (a b)"), 0.0), "ve")
                W("sync", "ve", ve_z0)
                for q4 in range(4):
                    t_xpre = LD(lambda e, q4=q4: e.dma_start(
                        out=xpre[:, q4 * 8:(q4 + 1) * 8, 3:259],
                        in_=xsp[:, q4 * 8:(q4 + 1) * 8, 0:TCH]), 5)
            else:
                for q4 in range(4):
                    t_xpre = LD(lambda e, col0=col0, q4=q4: e.dma_start(
                        out=xpre[:, q4 * 8:(q4 + 1) * 8, :],
                        in_=xsp[:, q4 * 8:(q4 + 1) * 8, col0 - 3: col0 + TCH]), 5)
            W("sync", "ve", ev_zread_ve)
            for q4 in range(4):
                t_z = LD(lambda e, col0=col0, q4=q4: e.dma_start(
                    out=zb[:, q4 * 8:(q4 + 1) * 8, :],
                    in_=zsp[:, q4 * 8:(q4 + 1) * 8, col0:col0 + TCH]), 6)
            t_z_last = t_z

            WT("vector", t_xpre)
            I("vector", lambda e: e.tensor_tensor(
                xc_[:], xpre[:, :, 0:TCH],
                convw_sb[:, :, 0:1].broadcast_to([128, KDI, TCH]), ALU.mult), "ve")
            for j in (1, 2, 3):
                for hh in range(4):
                    I("vector", lambda e, j=j, hh=hh: e.tensor_tensor(
                        cq_t[:], xpre[:, hh * 8:(hh + 1) * 8, j:j + TCH],
                        convw_sb[:, hh * 8:(hh + 1) * 8, j:j + 1].broadcast_to([128, 8, TCH]),
                        ALU.mult), "ve")
                    I("vector", lambda e, hh=hh: e.tensor_tensor(
                        xc_[:, hh * 8:(hh + 1) * 8, :],
                        xc_[:, hh * 8:(hh + 1) * 8, :], cq_t[:], ALU.add), "ve")
            I("vector", lambda e: e.tensor_tensor(
                xc_[:], xc_[:],
                convb_sb[:].unsqueeze(2).broadcast_to([128, KDI, TCH]), ALU.add), "ve")
            ve_conv = C["ve"]
            W("scalar", "ve", ve_conv)
            sc_silu = I("scalar", lambda e: e.activation(xcf, xcf, AF.Silu), "sc")
            WT("scalar", t_z_last)
            W("scalar", "ve", ev_zread_ve)
            sc_zsilu = I("scalar", lambda e: e.activation(
                zb[:].rearrange("p a b -> p (a b)"),
                zb[:].rearrange("p a b -> p (a b)"), AF.Silu), "sc")
            W("vector", "sc", sc_silu)
            ve_xcb = I("vector", lambda e: e.tensor_copy(xcbf, xcf), "ve")

            p_xd = pb[4][:, 0:TCH]
            p_bc = pb[4][0:32, TCH:2 * TCH]
            W("tensor", "ve", ve_xcb)
            for k in range(KDI):
                I("tensor", lambda e, k=k: e.matmul(
                    p_xd, wx_bf[:, k, 0:128], xcb[:, k, :],
                    start=(k == 0), stop=(k == KDI - 1)), "te")
            for k in range(KDI):
                I("tensor", lambda e, k=k: e.matmul(
                    p_bc, wx_bf[:, k, 128:160], xcb[:, k, :],
                    start=(k == 0), stop=(k == KDI - 1)), "te")
            te_xp = C["te"]
            W("scalar", "te", te_xp)
            I("scalar", lambda e: e.activation(dtlo[:], p_xd, AF.Copy), "sc")
            sc_xp = I("scalar", lambda e: e.activation(bc_sb[:], p_bc, AF.Copy), "sc")

            wave_sc = sc_xp
            slots = [pb[4 + s // 2][:, (s % 2) * TCH:(s % 2 + 1) * TCH]
                     for s in range(8)]
            for w in range(8):
                W("tensor", "sc", wave_sc)
                te_w = 0
                for s in range(8):
                    j = w * 8 + s
                    if j < KDI:
                        te_w = I("tensor", lambda e, j=j, s=s: e.matmul(
                            slots[s], wdt_bf[:, j * 128:(j + 1) * 128], dtlo[:],
                            start=True, stop=True), "te")
                    else:
                        n = j - KDI
                        te_w = I("tensor", lambda e, n=n, s=s: e.matmul(
                            slots[s], oneh[:, n * 128:(n + 1) * 128], bc_sb[:],
                            start=True, stop=True), "te")
                W("scalar", "te", te_w)
                for s in range(8):
                    j = w * 8 + s
                    if j < KDI:
                        wave_sc = I("scalar", lambda e, j=j, s=s: e.activation(
                            delta[:, j, :], slots[s], AF.Identity,
                            bias=dtb_sb[:, j:j + 1]), "sc")
                    else:
                        n = j - KDI
                        dst = b_all if n < 16 else c_all
                        dn = n if n < 16 else n - 16
                        wave_sc = I("scalar", lambda e, dst=dst, dn=dn, s=s: e.activation(
                            dst[:, dn, :], slots[s], AF.Copy), "sc")

            ve_sp_add = 0
            for hh in range(4):
                dsl = deltaf.rearrange("p (a b) -> p a b", b=2048)[:, hh, :]
                W("scalar", "ve", ve_sp_add)
                I("scalar", lambda e, dsl=dsl: e.activation(spq_t[:], dsl, AF.Abs), "sc")
                I("scalar", lambda e: e.activation(spq_t[:], spq_t[:], AF.Exp, scale=-1.0), "sc")
                I("scalar", lambda e: e.activation(spq_t[:], spq_t[:], AF.Ln, bias=1.0), "sc")
                sc_sp = I("scalar", lambda e, dsl=dsl: e.activation(dsl, dsl, AF.Relu), "sc")
                W("vector", "sc", sc_sp)
                ve_sp_add = I("vector", lambda e, dsl=dsl: e.tensor_tensor(dsl, dsl, spq_t[:], ALU.add), "ve")
            ve_pre = C["ve"]

            fin_state = {"yg_ve": 0, "t_yg8": t_yg8[0]}

            def emit_dxdt_dbx(dt):
                I("vector", lambda e, dt=dt: e.tensor_tensor(
                    dxdtb[:], delta[:, dt, :], xc_[:, dt, :], ALU.mult), "ve")
                return I("vector", lambda e: e.tensor_tensor(
                    dbx_b[:], dxdtb[:].unsqueeze(1).broadcast_to([128, DS, TCH]),
                    b_all[:], ALU.mult), "ve")

            def emit_exps(dt, ve_dbx):
                for n in range(DS):
                    W("scalar", "ve", ve_dbx)
                    exp_sc[n] = I("scalar", lambda e, dt=dt, n=n: e.activation(
                        dAs_t[n][:], delta[:, dt, :], AF.Exp,
                        scale=a_sb[:, dt, n:n + 1]), "sc")

            ve_dbx = emit_dxdt_dbx(0)
            emit_exps(0, ve_dbx)
            for dt in range(KDI):
                g = dt % 8
                unit_mm(cI * KH + 2 * dt)
                unit_mm(cI * KH + 2 * dt + 1)
                for n in range(DS):
                    W("vector", "sc", exp_sc[n])
                    I("vector", lambda e, n=n, dt=dt: e.tensor_tensor_scan(
                        hall[:, n, :], dAs_t[n][:], dbx_b[:, n, :], hcar[:, dt, n:n + 1],
                        ALU.mult, ALU.add), "ve", nodep=(n > 0))
                sl_ve = C["ve"]
                I("vector", lambda e: e.tensor_tensor(ct[:], hall[:], c_all[:], ALU.mult), "ve")
                I("vector", lambda e: e.tensor_tensor(
                    ct[:, 0:8, :], ct[:, 0:8, :], ct[:, 8:16, :], ALU.add), "ve")
                I("vector", lambda e: e.tensor_tensor(
                    ct[:, 0:4, :], ct[:, 0:4, :], ct[:, 4:8, :], ALU.add), "ve")
                I("vector", lambda e: e.tensor_tensor(
                    ct[:, 0:2, :], ct[:, 0:2, :], ct[:, 2:4, :], ALU.add), "ve")
                I("vector", lambda e: e.tensor_tensor(
                    ydtb[:], ct[:, 0, :], ct[:, 1, :], ALU.add), "ve")
                I("vector", lambda e, dt=dt: e.scalar_tensor_tensor(
                    ydtb[:], xc_[:, dt, :], d_sb[:, dt:dt + 1], ydtb[:],
                    ALU.mult, ALU.add), "ve")
                if dt == 0:
                    W("vector", "sc", sc_zsilu)
                if g == 0:
                    WT("vector", fin_state["t_yg8"])
                fin_state["yg_ve"] = I("vector", lambda e, g=g, dt=dt: e.tensor_tensor(
                    yg8[:, g, :], ydtb[:], zb[:, dt, :], ALU.mult), "ve")
                if g == 7:
                    gi = dt // 8
                    ve_g = C["ve"]
                    W("gpsimd", "ve", ve_g)
                    fin_state["t_yg8"] = ST(lambda e, w4=w4, coff=coff, gi=gi: e.dma_start(
                        out=ygsp[:, w4, gi * 8:(gi + 1) * 8, coff:coff + TCH],
                        in_=yg8[:]), 5)
                if dt + 1 < KDI:
                    ve_dbx = emit_dxdt_dbx(dt + 1)
                if dt > 0:
                    unit_pst(cI * KH + 2 * dt - 2)
                    unit_pst(cI * KH + 2 * dt - 1)
                W("scalar", "ve", sl_ve)
                I("scalar", lambda e, dt=dt: e.activation(
                    hcar[:, dt, :], hall[:, :, TCH - 1], AF.Copy), "sc")
                if dt + 1 < KDI:
                    emit_exps(dt + 1, ve_dbx)
                unit_silu(cI * KH + 2 * dt)
                unit_silu(cI * KH + 2 * dt + 1)
            unit_pst(cI * KH + 62)
            unit_pst(cI * KH + 63)
            t_yg8[0] = fin_state["t_yg8"]
            ev_zread_ve = fin_state["yg_ve"]
            ev_xpre_ve = C["ve"]
            ev_xpre_gp = C["gp"]
        snap_p3 = st_snap()
        ve_p3 = C["ve"]
        sc_p3 = C["sc"]
        gp_p3 = C["gp"]

        snap_p5 = snap_p3
        ve_p5 = ve_p3
        sc_p5 = sc_p3
        MARKS["P5"] = {k: len(v) for k, v in streams.items()}

        wdb = vb(0, KH, 512)
        pwin = vb(16384, KH, 512)
        wd_big = [vf(32768, 4, 512), vf(34816, 4, 512)]
        wdT_t = wdT.rearrange("(k p) l -> p k l", p=128)
        ostf = vf(36864, 2, 512)
        t_ost2 = [None, None]
        te_p6a_grp = 0
        ev_bank6 = [sc_p3] * 4
        ev_sgn = [0, 0]
        W("tensor", "ve", ve_p3)
        for mg in range(4):
            for q in range(16):
                sl = q % 2
                if mg == 0 and q == 0:
                    W("sync", "ve", ve_p5)
                    W("sync", "sc", sc_p5)
                    for tok in snap_p3:
                        WT("sync", tok)
                W("sync", "sc", ev_sgn[sl])
                t_wd = LD(lambda e, q=q, mg=mg, sl=sl: e.dma_start(
                    out=wd_big[sl][:],
                    in_=wdT_t[:, q * 4:(q + 1) * 4, mg * 512:(mg + 1) * 512]), 10 + sl)
                WT("scalar", t_wd)
                if mg == 0 and q == 0:
                    W("scalar", "ve", ve_p5)
                if q == 0:
                    W("scalar", "te", te_p6a_grp)
                ev_sgn[sl] = I("scalar", lambda e, q=q, sl=sl: e.activation(
                    wdb[:, q * 4:(q + 1) * 4, :].rearrange("p a b -> p (a b)"),
                    wd_big[sl][:].rearrange("p a b -> p (a b)"), AF.Sign), "sc")
            sc_wd = C["sc"]
            for wdw in range(4):
                if mg == 0 and wdw == 0:
                    for tok in snap_p5:
                        WT("sync", tok)
                    W("sync", "ve", ve_p5)
                W("sync", "te", te_p6a_grp)
                t_pw = LD(lambda e, w=wdw: e.dma_start(
                    out=pwin[:], in_=psp[:, w, :, :]), 12)
                for ms in range(4):
                    bank = (ms + wdw) % 4
                    W("tensor", "sc", sc_wd)
                    WT("tensor", t_pw)
                    W("tensor", "sc", ev_bank6[bank])
                    for k in range(KH):
                        I("tensor", lambda e, k=k, ms=ms, bank=bank: e.matmul(
                            pb[bank][:], wdb[:, k, ms * 128:(ms + 1) * 128], pwin[:, k, :],
                            start=(k == 0), stop=(k == KH - 1)), "te")
                    te_mm = C["te"]
                    W("scalar", "te", te_mm)
                    osl = (ms + 4 * wdw) % 2
                    WT("scalar", t_ost2[osl])
                    sc_ost = I("scalar", lambda e, bank=bank, osl=osl: e.activation(
                        ostf[:, osl, :], pb[bank][:], AF.Copy), "sc")
                    ev_bank6[bank] = sc_ost
                    W("gpsimd", "sc", sc_ost)
                    t_ost2[osl] = ST(lambda e, mg=mg, ms=ms, w=wdw, osl=osl: e.dma_start(
                        out=offn[:, mg * 4 + ms, w * 512:(w + 1) * 512], in_=ostf[:, osl, :]), 1 + 3 * osl)
                te_p6a_grp = C["te"]
        snap_p6a = st_snap()
        sc_p6a = C["sc"]
        te_p6a = C["te"]

        wob = vb(0, KDI, 512)
        ygwin = vb(8192, KDI, 512)
        wo_big = [vf(16384, 8, 512), vf(20480, 8, 512)]
        xct2 = [vf(24576, 512), vf(25088, 512)]
        oft2 = [vf(25600, 512), vf(26112, 512)]
        ostg2 = [vf(26624, 512), vf(27136, 512)]
        woT_t = woT.rearrange("(k p) l -> p k l", p=128)
        xcol_t = xcol.rearrange("(t p) l -> p t l", p=128)
        ev_bank_cmb = [0, 0, 0, 0]
        ev_cmb_slot = [0, 0]
        t_out2 = [None, None]
        t_xct2 = [None, None]
        t_oft2 = [None, None]
        te_p6b_grp = 0
        ev_cst = [0, 0]
        for mg in range(4):
            for q in range(4):
                sl = q % 2
                if mg == 0 and q == 0:
                    W("sync", "sc", sc_p6a)
                W("sync", "sc", ev_cst[sl])
                t_wo = LD(lambda e, q=q, mg=mg, sl=sl: e.dma_start(
                    out=wo_big[sl][:],
                    in_=woT_t[:, q * 8:(q + 1) * 8, mg * 512:(mg + 1) * 512]), 10 + sl)
                WT("scalar", t_wo)
                if mg == 0 and q == 0:
                    W("scalar", "sc", sc_p6a)
                if q == 0:
                    W("scalar", "te", te_p6b_grp)
                ev_cst[sl] = I("scalar", lambda e, q=q, sl=sl: e.activation(
                    wob[:, q * 8:(q + 1) * 8, :].rearrange("p a b -> p (a b)"),
                    wo_big[sl][:].rearrange("p a b -> p (a b)"), AF.Copy), "sc")
            sc_wo = C["sc"]
            for wdw in range(4):
                if mg == 0 and wdw == 0:
                    for tok in snap_p3:
                        WT("sync", tok)
                    W("sync", "sc", sc_p6a)
                W("sync", "te", max(te_p6a if (mg == 0 and wdw == 0) else 0, te_p6b_grp))
                t_yg = LD(lambda e, w=wdw: e.dma_start(
                    out=ygwin[:], in_=ygsp[:, w, :, :]), 12)
                for ms in range(4):
                    bank = (ms + wdw) % 4
                    gidx = ms + 4 * wdw
                    osl = gidx % 2
                    row = mg * 4 + ms
                    W("tensor", "sc", sc_wo)
                    WT("tensor", t_yg)
                    W("tensor", "ve", ev_bank_cmb[bank])
                    for k in range(KDI):
                        I("tensor", lambda e, k=k, ms=ms, bank=bank: e.matmul(
                            pb[bank][:], wob[:, k, ms * 128:(ms + 1) * 128], ygwin[:, k, :],
                            start=(k == 0), stop=(k == KDI - 1)), "te")
                    te_mm = C["te"]
                    W("sync", "ve", ev_cmb_slot[osl])
                    if mg == 0 and wdw == 0 and ms == 0:
                        for tok in snap_p6a:
                            WT("sync", tok)
                    t_xct2[osl] = LD(lambda e, row=row, w=wdw, osl=osl: e.dma_start(
                        out=xct2[osl], in_=xcol_t[:, row, w * 512:(w + 1) * 512]), 5 + 2 * osl)
                    t_oft2[osl] = LD(lambda e, row=row, w=wdw, osl=osl: e.dma_start(
                        out=oft2[osl], in_=offn[:, row, w * 512:(w + 1) * 512]), 6 + 8 * osl)
                    W("vector", "te", te_mm)
                    WT("vector", t_xct2[osl], t_oft2[osl])
                    WT("vector", t_out2[osl])
                    I("vector", lambda e, bank=bank, osl=osl: e.tensor_tensor(
                        ostg2[osl], pb[bank][:], xct2[osl], ALU.add), "ve")
                    ev_cmb_slot[osl] = I("vector", lambda e, osl=osl: e.tensor_tensor(
                        ostg2[osl], ostg2[osl], oft2[osl], ALU.add), "ve")
                    ev_bank_cmb[bank] = ev_cmb_slot[osl]
                    W("gpsimd", "ve", ev_cmb_slot[osl])
                    t_out2[osl] = ST(lambda e, row=row, w=wdw, osl=osl: e.dma_start(
                        out=outT[:, row, w * 512:(w + 1) * 512], in_=ostg2[osl]), 2 + osl)
                te_p6b_grp = C["te"]
        for tok in st_snap():
            WT("gpsimd", tok)
        if CUT is not None:
            cut = MARKS[CUT]
            for k in streams:
                streams[k] = streams[k][:cut[k]]

        @block.sync
        def _(eng):
            for f in streams["sync"]:
                f(eng)

        @block.tensor
        def _(eng):
            for f in streams["tensor"]:
                f(eng)

        @block.vector
        def _(eng):
            for f in streams["vector"]:
                f(eng)

        @block.scalar
        def _(eng):
            for f in streams["scalar"]:
                f(eng)

        @block.gpsimd
        def _(eng):
            for f in streams["gpsimd"]:
                f(eng)

    return nc


def kernel(L0_attn_hidden_states, L0_attn_in_proj_w, L0_attn_conv_w,
           L0_attn_conv_b, L0_attn_x_proj_w, L0_attn_dt_proj_w,
           L0_attn_dt_proj_b, L0_attn_A_log, L0_attn_D, L0_attn_out_proj_w,
           L0_ffn_W_g, L0_ffn_W_u, L0_ffn_W_d):
    x = np.ascontiguousarray(np.asarray(L0_attn_hidden_states, dtype=np.float32))
    B = x.shape[0]

    def t(a):
        return np.ascontiguousarray(np.asarray(a, dtype=np.float32).T)

    def perpart(a):
        a = np.asarray(a, dtype=np.float32)
        sh = a.shape[1:]
        return np.ascontiguousarray(
            a.reshape(KDI, 128, *sh).transpose(1, 0, *range(2, 2 + len(sh))))

    winT = t(L0_attn_in_proj_w)
    wxT = np.ascontiguousarray(
        np.asarray(L0_attn_x_proj_w, np.float32).T.reshape(KDI, 128, 160).transpose(1, 0, 2))
    wdtT = t(L0_attn_dt_proj_w)
    woT = t(L0_attn_out_proj_w)
    wgT = t(L0_ffn_W_g)
    wuT = t(L0_ffn_W_u)
    wdT = t(L0_ffn_W_d)
    convw_h = perpart(L0_attn_conv_w)
    convb_h = perpart(L0_attn_conv_b)
    dtb_h = perpart(L0_attn_dt_proj_b)
    alog_h = perpart(L0_attn_A_log)
    dvec_h = perpart(L0_attn_D)

    if "nc" not in _CACHE:
        _CACHE["nc"] = build_program()
    nc = _CACHE["nc"]

    shared = dict(winT=winT, wxT=wxT, wdtT=wdtT, woT=woT, wgT=wgT, wuT=wuT,
                  wdT=wdT, convw=convw_h, convb=convb_h, dtb=dtb_h,
                  alog=alog_h, dvec=dvec_h)
    in_maps = []
    for b in range(B):
        m = dict(shared)
        m["xrow"] = np.ascontiguousarray(x[b])
        m["xcol"] = np.ascontiguousarray(x[b].T)
        in_maps.append(m)

    res = run_bass_kernel_spmd(nc, in_maps, core_ids=list(range(B)))
    outs = []
    for b in range(B):
        arr = np.asarray(res.results[b]["outT"])
        outs.append(arr.transpose(2, 1, 0).reshape(L, D))
    return np.stack(outs).astype(np.float32)


# version 18 (final)
# speedup vs baseline: 1.0317x; 1.0317x over previous
import contextlib

import numpy as np

import concourse.bass as bass
import concourse.mybir as mybir
from concourse.bass_utils import run_bass_kernel_spmd

F32 = mybir.dt.float32
BF16 = mybir.dt.bfloat16
AF = mybir.ActivationFunctionType
ALU = mybir.AluOpType
AX = mybir.AxisListType

L = 2048
D = 2048
DI = 4096
DS = 16
H = 8192
TCH = 256
NCH = L // TCH
NT = 16
KD = D // 128
KDI = DI // 128
KH = H // 128
NLD = 17
NST = 8

_CACHE = {}
CUT = None


def build_program():
    nc = bass.Bass("TRN2", target_bir_lowering=False)

    xrow = nc.dram_tensor("xrow", [L, D], F32, kind="ExternalInput").ap()
    xcol = nc.dram_tensor("xcol", [D, L], F32, kind="ExternalInput").ap()
    winT = nc.dram_tensor("winT", [D, 2 * DI], F32, kind="ExternalInput").ap()
    wxT = nc.dram_tensor("wxT", [128, KDI, 160], F32, kind="ExternalInput").ap()
    wdtT = nc.dram_tensor("wdtT", [128, DI], F32, kind="ExternalInput").ap()
    woT = nc.dram_tensor("woT", [DI, D], F32, kind="ExternalInput").ap()
    wgT = nc.dram_tensor("wgT", [D, H], F32, kind="ExternalInput").ap()
    wuT = nc.dram_tensor("wuT", [D, H], F32, kind="ExternalInput").ap()
    wdT = nc.dram_tensor("wdT", [H, D], F32, kind="ExternalInput").ap()
    convw = nc.dram_tensor("convw", [128, KDI, 4], F32, kind="ExternalInput").ap()
    convb = nc.dram_tensor("convb", [128, KDI], F32, kind="ExternalInput").ap()
    dtb = nc.dram_tensor("dtb", [128, KDI], F32, kind="ExternalInput").ap()
    alog = nc.dram_tensor("alog", [128, KDI, DS], F32, kind="ExternalInput").ap()
    dvec = nc.dram_tensor("dvec", [128, KDI], F32, kind="ExternalInput").ap()

    outT = nc.dram_tensor("outT", [128, KD, L], F32, kind="ExternalOutput").ap()
    wgusp = nc.dram_tensor("wgusp", [128, KH, 4096], BF16, kind="Internal").ap()
    xsp = nc.dram_tensor("xsp", [128, KDI, L], BF16, kind="Internal").ap()
    zsp = nc.dram_tensor("zsp", [128, KDI, L], BF16, kind="Internal").ap()
    qsp = nc.dram_tensor("qsp", [128, KD, L], BF16, kind="Internal").ap()
    ygsp = nc.dram_tensor("ygsp", [128, 4, KDI, 512], BF16, kind="Internal").ap()
    psp = nc.dram_tensor("psp", [128, 4, KH, 512], BF16, kind="Internal").ap()
    offn = nc.dram_tensor("offn", [128, KD, L], F32, kind="Internal").ap()

    streams = {k: [] for k in ("sync", "tensor", "vector", "scalar", "gpsimd")}
    C = {"ve": 0, "sc": 0, "te": 0, "gp": 0}
    ld_cnt = [0] * NLD
    st_cnt = [0] * NST
    ld_rr = [0]
    st_rr = [0]

    with contextlib.ExitStack() as ctx:
        sems = {name: ctx.enter_context(nc.semaphore(name)) for name in C}
        ldsem = [ctx.enter_context(nc.semaphore(f"la{i}")) for i in range(NLD)]
        stsem = [ctx.enter_context(nc.semaphore(f"sa{i}")) for i in range(NST)]

        def W(eng, sem, val):
            if val and val > 0:
                streams[eng].append(lambda e, s=sems[sem], v=val: e.wait_ge(s, v))

        _SELF = {"vector": "ve", "scalar": "sc", "gpsimd": "gp"}

        def I(eng, builder, sem=None, amt=1, nodep=False):
            own = _SELF.get(eng)
            if nodep:
                own = None
            if own is not None and C[own] > 0 and sem is not None:
                streams[eng].append(
                    lambda e, s=sems[own], v=C[own]: e.wait_ge(s, v))
            if sem is not None:
                C[sem] += amt
                streams[eng].append(
                    lambda e, b=builder, s=sems[sem], a=amt: b(e).then_inc(s, a))
            else:
                streams[eng].append(lambda e, b=builder: b(e))
            return C[sem] if sem else None

        def LD(builder, lane):
            i = lane
            ld_cnt[i] += 16
            streams["sync"].append(
                lambda e, b=builder, s=ldsem[i]: b(e).then_inc(s, 16))
            return ("L", i, ld_cnt[i])

        def ST(builder, lane):
            i = lane
            st_cnt[i] += 16
            streams["gpsimd"].append(
                lambda e, b=builder, s=stsem[i]: b(e).then_inc(s, 16))
            return ("S", i, st_cnt[i])

        def LDG(builder, lane):
            i = lane
            ld_cnt[i] += 16
            streams["gpsimd"].append(
                lambda e, b=builder, s=ldsem[i]: b(e).then_inc(s, 16))
            return ("L", i, ld_cnt[i])

        def WT(eng, *toks):
            for tok in toks:
                if tok is None:
                    continue
                kind, i, v = tok
                s = ldsem[i] if kind == "L" else stsem[i]
                streams[eng].append(lambda e, s=s, v=v: e.wait_ge(s, v))

        def st_snap():
            return [("S", i, st_cnt[i]) for i in range(NST) if st_cnt[i]]

        PA = ctx.enter_context(nc.sbuf_tensor("PA", [128, 40960], F32))
        PAb = PA[:].bitcast(BF16)

        oneh = ctx.enter_context(nc.sbuf_tensor("oneh", [32, 32 * 128], BF16))
        wx_bf = ctx.enter_context(nc.sbuf_tensor("wx_bf", [128, KDI, 160], BF16))
        wdt_bf = ctx.enter_context(nc.sbuf_tensor("wdt_bf", [128, DI], BF16))
        a_sb = ctx.enter_context(nc.sbuf_tensor("a_sb", [128, KDI, DS], F32))
        convw_sb = ctx.enter_context(nc.sbuf_tensor("convw_sb", [128, KDI, 4], F32))
        convb_sb = ctx.enter_context(nc.sbuf_tensor("convb_sb", [128, KDI], F32))
        dtb_sb = ctx.enter_context(nc.sbuf_tensor("dtb_sb", [128, KDI], F32))
        d_sb = ctx.enter_context(nc.sbuf_tensor("d_sb", [128, KDI], F32))
        hcar = ctx.enter_context(nc.sbuf_tensor("hcar", [128, KDI, DS], F32))
        dtlo = ctx.enter_context(nc.sbuf_tensor("dtlo", [128, TCH], BF16))
        bc_sb = ctx.enter_context(nc.sbuf_tensor("bc_sb", [32, TCH], BF16))
        dAs_t = [ctx.enter_context(nc.sbuf_tensor(f"dA{i}", [128, TCH], F32))
                 for i in range(15)]
        stat = ctx.enter_context(nc.sbuf_tensor("stat", [128, 8 * NT], F32))
        identb = ctx.enter_context(nc.sbuf_tensor("identb", [128, 128], BF16))
        cst = ctx.enter_context(nc.sbuf_tensor("cst", [128, 2], F32))
        eps5 = cst[:, 0:1]
        eps8 = cst[:, 1:2]

        pb = [ctx.enter_context(nc.psum_tensor(f"pb{i}", [128, 512], F32))
              for i in range(8)]

        block = ctx.enter_context(nc.Block())

        def vf(off, *shape):
            n = int(np.prod(shape))
            ap = PA[:, off:off + n]
            if len(shape) == 2:
                ap = ap.rearrange("p (a b) -> p a b", b=shape[-1])
            elif len(shape) == 3:
                ap = ap.rearrange("p (a b c) -> p a b c", b=shape[1], c=shape[2])
            return ap

        def vb(off_f32, *shape):
            n = int(np.prod(shape))
            ap = PAb[:, 2 * off_f32: 2 * off_f32 + n]
            if len(shape) == 3:
                ap = ap.rearrange("p (a b c) -> p a b c", b=shape[1], c=shape[2])
            elif len(shape) == 2:
                ap = ap.rearrange("p (a b) -> p a b", b=shape[-1])
            return ap

        wx_stage = PA[:, 0:KDI * 160]
        wdt_stage = PA[:, 5120:5120 + DI]
        t_wx = LD(lambda e: e.dma_start(out=wx_stage, in_=wxT.rearrange("p a b -> p (a b)")), 0)
        t_wdt = LD(lambda e: e.dma_start(out=wdt_stage, in_=wdtT), 0)
        t_a = LD(lambda e: e.dma_start(out=a_sb[:], in_=alog), 0)
        t_cw = LD(lambda e: e.dma_start(out=convw_sb[:], in_=convw), 0)
        t_cb = LD(lambda e: e.dma_start(out=convb_sb[:], in_=convb), 0)
        t_db = LD(lambda e: e.dma_start(out=dtb_sb[:], in_=dtb), 0)
        t_dv = LD(lambda e: e.dma_start(out=d_sb[:], in_=dvec), 0)

        WT("vector", t_dv)
        I("vector", lambda e: e.tensor_copy(
            wx_bf[:].rearrange("p a b -> p (a b)"), wx_stage), "ve")
        I("vector", lambda e: e.tensor_copy(wdt_bf[:], wdt_stage), "ve")
        WT("scalar", t_dv)
        sc_aexp = I("scalar", lambda e: e.activation(
            a_sb[:].rearrange("p a b -> p (a b)"),
            a_sb[:].rearrange("p a b -> p (a b)"), AF.Exp), "sc")
        W("vector", "sc", sc_aexp)
        I("vector", lambda e: e.tensor_scalar(
            a_sb[:].rearrange("p a b -> p (a b)"),
            a_sb[:].rearrange("p a b -> p (a b)"), -1.0, None, ALU.mult), "ve")
        I("vector", lambda e: e.memset(cst[:, 0:1], 1e-5), "ve")
        I("vector", lambda e: e.memset(cst[:, 1:2], 1e-8), "ve")
        tmpI = PA[:, 16384:16384 + 128]
        tmpO = PA[0:32, 16512:16512 + 4096]
        gp_o = None
        I("gpsimd", lambda e: e.iota(tmpI, [[-1, 128]], base=0, channel_multiplier=1,
                                     allow_small_or_imprecise_dtypes=True), "gp")
        gp_o = I("gpsimd", lambda e: e.iota(tmpO, [[-1, 32], [0, 128]], base=0,
                                            channel_multiplier=1,
                                            allow_small_or_imprecise_dtypes=True), "gp")
        W("vector", "gp", gp_o)
        I("vector", lambda e: e.tensor_single_scalar(identb[:], tmpI, 0.0, ALU.is_equal), "ve")
        I("vector", lambda e: e.tensor_single_scalar(oneh[:], tmpO, 0.0, ALU.is_equal), "ve")
        I("vector", lambda e: e.memset(hcar[:].rearrange("p a b -> p (a b)"), 0.0), "ve")
        ve_p0 = C["ve"]
        MARKS = {}
        MARKS["P0"] = {k: len(v) for k, v in streams.items()}

        xr = [vf(0, 2048), vf(2048, 2048)]
        t0 = vf(8192, 2048)
        tq = vf(10240, 2048)
        mk = vf(12288, 2048)
        xntok = vb(14336, 2048)
        xqtok = vb(15360, 2048)
        qtt = vb(16384, NT, 128)
        xnT = vb(18432, KD, 2048)
        bnb = vf(4096, NT, 24)
        magg = vf(4480, NT, 2)
        stat2 = vf(4512, 3, NT)
        xrow_t = xrow.rearrange("(t p) d -> p t d", p=128)

        def scol(j):
            return stat[:, j * NT:(j + 1) * NT]

        ev_xrA = [0, 0]
        for tt in range(NT):
            sl = tt % 2
            W("sync", "ve", max(ev_xrA[sl], ve_p0 if tt == 0 else 0))
            t_xr = LD(lambda e, tt=tt, sl=sl: e.dma_start(out=xr[sl], in_=xrow_t[:, tt, :]), 1 + sl)
            WT("vector", t_xr)
            for bq in range(4):
                I("vector", lambda e, tt=tt, sl=sl, bq=bq: e.bn_stats(
                    bnb[:, tt, bq * 6:(bq + 1) * 6],
                    xr[sl].rearrange("p (a b) -> p a b", b=512)[:, bq, :]), "ve")
            I("vector", lambda e, tt=tt: e.bn_aggr(magg[:, tt, :], bnb[:, tt, :]), "ve")
            I("vector", lambda e, tt=tt, sl=sl: e.reduce_max(
                stat2[:, 0, tt:tt + 1], xr[sl], AX.X), "ve")
            ev_xrA[sl] = I("vector", lambda e, tt=tt, sl=sl: e.tensor_reduce(
                stat2[:, 1, tt:tt + 1], xr[sl], AX.X, ALU.min), "ve")

        m_s = magg[:, :, 0]
        v_s = magg[:, :, 1]
        r_s, r2_s, rs_s, c2_s, nmr_s, b2_s = (scol(2), scol(3), scol(4),
                                              scol(5), scol(6), scol(7))
        mx_s, mn_s, rs128_s = stat2[:, 0, :], stat2[:, 1, :], stat2[:, 2, :]
        t6_s, t7_s = scol(0), scol(1)
        ve_A = C["ve"]
        W("scalar", "ve", ve_A)
        I("scalar", lambda e: e.activation(t6_s, v_s, AF.Ln, bias=eps5), "sc")
        sc_r = I("scalar", lambda e: e.activation(r_s, t6_s, AF.Exp, scale=-0.5), "sc")
        W("vector", "sc", sc_r)
        I("vector", lambda e: e.tensor_tensor(t6_s, v_s, r_s, ALU.mult), "ve")
        ve_v2 = I("vector", lambda e: e.tensor_tensor(t6_s, t6_s, r_s, ALU.mult), "ve")
        W("scalar", "ve", ve_v2)
        I("scalar", lambda e: e.activation(t6_s, t6_s, AF.Ln, bias=eps8), "sc")
        sc_r2 = I("scalar", lambda e: e.activation(r2_s, t6_s, AF.Exp, scale=-0.5), "sc")
        W("vector", "sc", sc_r2)
        I("vector", lambda e: e.tensor_tensor(t6_s, mx_s, m_s, ALU.subtract), "ve")
        I("vector", lambda e: e.tensor_tensor(t7_s, m_s, mn_s, ALU.subtract), "ve")
        I("vector", lambda e: e.tensor_tensor(t6_s, t6_s, t7_s, ALU.max), "ve")
        I("vector", lambda e: e.tensor_tensor(t6_s, t6_s, r_s, ALU.mult), "ve")
        I("vector", lambda e: e.tensor_tensor(t6_s, t6_s, r2_s, ALU.mult), "ve")
        I("vector", lambda e: e.reciprocal(t7_s, t6_s), "ve")
        I("vector", lambda e: e.tensor_scalar(rs_s, t6_s, 1.0 / 127.0, None, ALU.mult), "ve")
        I("vector", lambda e: e.tensor_scalar(rs128_s, rs_s, 128.0, None, ALU.mult), "ve")
        I("vector", lambda e: e.tensor_scalar(t7_s, t7_s, 127.0, None, ALU.mult), "ve")
        I("vector", lambda e: e.tensor_tensor(c2_s, r_s, r2_s, ALU.mult), "ve")
        I("vector", lambda e: e.tensor_tensor(c2_s, c2_s, t7_s, ALU.mult), "ve")
        I("vector", lambda e: e.tensor_tensor(nmr_s, m_s, r_s, ALU.mult), "ve")
        I("vector", lambda e: e.tensor_scalar(nmr_s, nmr_s, -1.0, None, ALU.mult), "ve")
        I("vector", lambda e: e.tensor_tensor(b2_s, m_s, c2_s, ALU.mult), "ve")
        ve_stats = I("vector", lambda e: e.tensor_scalar(
            b2_s, b2_s, -1.0, 128.5, ALU.mult, ALU.add), "ve")

        ev_xr_sc = [0, 0]
        ev_tq = [0] * 8
        ev_tok_te = 0
        t_qtt = None
        for tt in range(NT):
            sl = tt % 2
            W("sync", "sc", ev_xr_sc[sl])
            W("sync", "ve", ve_stats if tt < 2 else 0)
            t_xr = LD(lambda e, tt=tt, sl=sl: e.dma_start(out=xr[sl], in_=xrow_t[:, tt, :]), 1 + sl)
            WT("scalar", t_xr)
            if tt == 0:
                W("scalar", "ve", ve_stats)
            W("scalar", "te", ev_tok_te)
            I("scalar", lambda e, sl=sl, tt=tt: e.activation(
                xntok, xr[sl], AF.Identity, scale=r_s[:, tt:tt + 1],
                bias=nmr_s[:, tt:tt + 1]), "sc")
            sc_t0 = I("scalar", lambda e, sl=sl, tt=tt: e.activation(
                t0, xr[sl], AF.Identity, scale=c2_s[:, tt:tt + 1],
                bias=b2_s[:, tt:tt + 1]), "sc")
            ev_xr_sc[sl] = sc_t0
            W("vector", "sc", sc_t0)
            I("vector", lambda e: e.tensor_copy(tq, t0), "ve")
            for cbit in (128.0, 64.0, 32.0, 16.0, 8.0, 4.0, 2.0, 1.0):
                I("vector", lambda e, c=cbit: e.tensor_single_scalar(mk, tq, c, ALU.is_ge), "ve")
                I("vector", lambda e, c=cbit: e.scalar_tensor_tensor(
                    tq, mk, -c, tq, ALU.mult, ALU.add), "ve")
            I("vector", lambda e: e.tensor_tensor(t0, t0, tq, ALU.subtract), "ve")
            ve_xq = I("vector", lambda e, tt=tt: e.tensor_scalar(
                xqtok, t0, rs_s[:, tt:tt + 1], rs128_s[:, tt:tt + 1],
                ALU.mult, ALU.subtract), "ve")
            for j in range(2 * KD):
                qn = j % 8
                bank_b = pb[qn][:].bitcast(BF16)
                quarter = bank_b[:, 0:128]
                W("tensor", "ve", ve_xq)
                W("tensor", "sc", ev_tq[qn])
                src_ = xntok if j < KD else xqtok
                dcol = j % KD
                te_t = I("tensor", lambda e, q=quarter, s=src_, dcol=dcol: e.transpose(
                    q, s.rearrange("p (a b) -> p a b", b=128)[:, dcol, :], identb[:]), "te")
                W("scalar", "te", te_t)
                if j == KD and tt > 0:
                    WT("scalar", t_qtt)
                if j < KD:
                    ev_tq[qn] = I("scalar", lambda e, q=quarter, dcol=dcol, tt=tt: e.activation(
                        xnT[:, dcol, tt * 128:(tt + 1) * 128], q, AF.Copy), "sc")
                else:
                    ev_tq[qn] = I("scalar", lambda e, q=quarter, dcol=dcol: e.activation(
                        qtt[:, dcol, :], q, AF.Copy), "sc")
            ev_tok_te = C["te"]
            sc_tile = C["sc"]
            W("gpsimd", "sc", sc_tile)
            for h2 in range(2):
                t_qtt = ST(lambda e, tt=tt, h2=h2: e.dma_start(
                    out=qsp[:, h2 * 8:(h2 + 1) * 8, tt * 128:(tt + 1) * 128],
                    in_=qtt[:, h2 * 8:(h2 + 1) * 8, :]), 0)
        ve_p1 = C["ve"]
        sc_p1 = C["sc"]
        snap_qsp = st_snap()

        wstage2 = [vf(0, 16, 512), vf(8192, 16, 512)]
        wbf = vb(34816, 16, 512)
        xzst = vb(38912, 2, 2048)
        winT_t = winT.rearrange("(k p) (g c) -> p k g c", p=128, c=512)

        wgT_t = wgT.rearrange("(k p) (h c) -> p k h c", p=128, c=128)
        wuT_t = wuT.rearrange("(k p) (h c) -> p k h c", p=128, c=128)
        cslot_f = [vf(16384 + 512 * i, 4, 128) for i in range(4)]
        cslot_b = [vb(16384 + 512 * i, 512) for i in range(4)]
        cst_tok = [None] * 512
        cld_tok = [None] * 512
        wgu_tok = [None] * KH
        csc = [0, 0, 0, 0]

        def cast_piece_load(p):
            ht, wu, h = p // 8, (p // 4) % 2, p % 4
            sl = p % 4
            if p >= 4:
                WT("sync", cst_tok[p - 4])
            src = wgT_t if wu == 0 else wuT_t
            cld_tok[p] = LD(lambda e, ht=ht, h=h, src=src, sl=sl: e.dma_start(
                out=cslot_f[sl][:], in_=src[:, h * 4:(h + 1) * 4, ht, :]), 8 + sl)

        def cast_piece_do(p):
            ht, wu, h = p // 8, (p // 4) % 2, p % 4
            sl = p % 4
            WT("scalar", cld_tok[p])
            csc[sl] = I("scalar", lambda e, sl=sl: e.activation(
                cslot_b[sl][:], cslot_f[sl][:].rearrange("p a b -> p (a b)"),
                AF.Sign), "sc")
            W("gpsimd", "sc", csc[sl])
            if p >= 2:
                WT("gpsimd", cst_tok[p - 2])
            cst_tok[p] = ST(lambda e, ht=ht, wu=wu, h=h, sl=sl: e.dma_start(
                out=wgusp[:, ht, (wu * 4 + h) * 512:(wu * 4 + h + 1) * 512],
                in_=cslot_b[sl][:]), 3 + p % 2)
            if p % 8 == 7:
                wgu_tok[ht] = (cst_tok[p - 1], cst_tok[p])
        ev_wst_ve = [ve_p1, ve_p1]
        ev_wbf_te = 0
        ev_bank_sc = [ve_p1] * 4
        W("tensor", "sc", sc_p1)
        t_row = [None, None]
        tok_w = [None, None]
        W("sync", "ve", ve_p1)
        W("sync", "sc", sc_p1)
        for mg in range(2):
            wsl = mg % 2
            t_w0 = LD(lambda e, mg=mg, wsl=wsl: e.dma_start(out=wstage2[wsl][:, 0:8, :], in_=winT_t[:, 0:8, mg, :]), 3 + 10 * wsl)
            t_w1 = LD(lambda e, mg=mg, wsl=wsl: e.dma_start(out=wstage2[wsl][:, 8:16, :], in_=winT_t[:, 8:16, mg, :]), 4 + 10 * wsl)
            tok_w[wsl] = (t_w0, t_w1)
        for tok in snap_qsp:
            WT("sync", tok)
        for p0 in range(4):
            cast_piece_load(p0)
        for mg in range(16):
            wsl = mg % 2
            WT("vector", *tok_w[wsl])
            W("vector", "te", ev_wbf_te)
            if mg == 0:
                for tok in snap_qsp:
                    WT("vector", tok)
            ev_cast = I("vector", lambda e, wsl=wsl: e.tensor_copy(
                wbf[:].rearrange("p a b -> p (a b)"), wstage2[wsl][:].rearrange("p a b -> p (a b)")), "ve")
            ev_wst_ve[wsl] = ev_cast
            if mg + 2 < 16:
                W("sync", "ve", ev_cast)
                t_w0 = LD(lambda e, mg=mg, wsl=wsl: e.dma_start(out=wstage2[wsl][:, 0:8, :], in_=winT_t[:, 0:8, mg + 2, :]), 3 + 10 * wsl)
                t_w1 = LD(lambda e, mg=mg, wsl=wsl: e.dma_start(out=wstage2[wsl][:, 8:16, :], in_=winT_t[:, 8:16, mg + 2, :]), 4 + 10 * wsl)
                tok_w[wsl] = (t_w0, t_w1)
            for ms in range(4):
                row = mg * 4 + ms
                xsl = ms % 2
                for wdw in range(4):
                    bank = (ms * 4 + wdw) % 4
                    W("tensor", "ve", ev_cast)
                    W("tensor", "ve", ev_bank_sc[bank])
                    for k in range(16):
                        I("tensor", lambda e, k=k, ms=ms, w=wdw, bank=bank: e.matmul(
                            pb[bank][:], wbf[:, k, ms * 128:(ms + 1) * 128],
                            xnT[:, k, w * 512:(w + 1) * 512],
                            start=(k == 0), stop=(k == 15)), "te")
                    te_mm = C["te"]
                    W("vector", "te", te_mm)
                    if wdw == 0:
                        WT("vector", t_row[xsl])
                    ev_bank_sc[bank] = I("vector", lambda e, xsl=xsl, w=wdw, bank=bank: e.tensor_copy(
                        xzst[:, xsl, w * 512:(w + 1) * 512], pb[bank][:]), "ve")
                sc_row = C["ve"]
                W("gpsimd", "ve", sc_row)
                dst = xsp if row < KDI else zsp
                drow = row if row < KDI else row - KDI
                t_row[xsl] = ST(lambda e, dst=dst, drow=drow, xsl=xsl: e.dma_start(
                    out=dst[:, drow, :], in_=xzst[:, xsl, :]), 1 + xsl)
                for p in range(8 * row, 8 * row + 8):
                    cast_piece_do(p)
                    if p + 4 < 512:
                        cast_piece_load(p + 4)
            ev_wbf_te = C["te"]
        snap_p2 = st_snap()
        te_p2 = C["te"]
        ve_p2c = C["ve"]
        sc_p2c = C["sc"]
        MARKS["P2"] = {k: len(v) for k, v in streams.items()}

        xc_ = vf(0, KDI, TCH)
        xcf = vf(0, KDI * TCH)
        delta = vb(8192, KDI, TCH)
        deltaf = vb(8192, KDI * TCH)
        wgu_sl = [vb(12288, 2, KD, 128), vb(30640, 2, KD, 128)]
        cq_t = vf(14336, 8, TCH)
        spq_t = vf(14336, 2048)
        xpre = vb(16384, KDI, 259)
        hall = vf(16384, DS, TCH)
        hallf = vf(16384, DS * TCH)
        ct = vb(20528, DS, TCH)
        qwin = vb(22576, KD, TCH)
        zb = vb(24624, KDI, TCH)
        yg8 = vb(28720, 8, TCH)
        dxdtb = vb(29744, TCH)
        pst2 = vb(29872, TCH)
        ydtb = vb(30000, TCH)
        sgb2 = [vb(30128, TCH), vb(30512, TCH)]
        xcb = vb(32768, KDI, TCH)
        xcbf = vb(32768, KDI * TCH)
        dbx_b = vb(32768, DS, TCH)
        b_all = vb(36864, DS, TCH)
        c_all = vb(38912, DS, TCH)

        dAs_t = dAs_t + [vf(30256, TCH)]
        ev_xpre_ve = ve_p1
        ev_xpre_gp = 0
        ev_zread_ve = 0
        te_xp = 0
        t_yg8 = [None, None, None, None]
        bank_sc = [sc_p2c] * 4
        bank_ve = [ve_p2c] * 4
        unit_te = {}
        silu_sc = {}
        wgu_te = [0, 0]
        wgu_ld_tok = [None, None]
        qwin_te = [te_p2]
        t_qw = [None, None]
        pst_tok = [None, None]
        sgb_ve = [0, 0]
        exp_sc = [0] * 16

        def wgu_load(ht, cI):
            sl = ht % 2
            W("sync", "te", wgu_te[sl])
            if cI == 0 and ht < 2:
                W("sync", "ve", ve_p2c)
                W("sync", "te", te_p2)
                WT("sync", *wgu_tok[ht])
            wgu_ld_tok[sl] = LD(lambda e, ht=ht, sl=sl: e.dma_start(
                out=wgu_sl[sl][:].rearrange("p a k c -> p (a k c)"),
                in_=wgusp[:, ht, :]), 15 + sl)

        def unit_mm(u):
            ht = u % KH
            sl = ht % 2
            bank = pb[u % 4]
            WT("tensor", t_qw[1])
            WT("tensor", wgu_ld_tok[sl])
            W("tensor", "sc", bank_sc[u % 4])
            W("tensor", "ve", bank_ve[u % 4])
            for k in range(KD):
                I("tensor", lambda e, k=k, bank=bank, sl=sl: e.matmul(
                    bank[:, 0:TCH], wgu_sl[sl][:, 0, k, :], qwin[:, k, :],
                    start=(k == 0), stop=(k == KD - 1)), "te")
            for k in range(KD):
                I("tensor", lambda e, k=k, bank=bank, sl=sl: e.matmul(
                    bank[:, TCH:2 * TCH], wgu_sl[sl][:, 1, k, :], qwin[:, k, :],
                    start=(k == 0), stop=(k == KD - 1)), "te")
            unit_te[u] = C["te"]
            wgu_te[sl] = C["te"]
            qwin_te[0] = C["te"]
            if ht + 2 < KH:
                wgu_load(ht + 2, u // KH)

        def unit_silu(u):
            bank = pb[u % 4]
            W("scalar", "te", unit_te[u])
            W("scalar", "ve", sgb_ve[u % 2])
            silu_sc[u] = I("scalar", lambda e, bank=bank, sl2=u % 2: e.activation(
                sgb2[sl2][:], bank[:, 0:TCH], AF.Silu), "sc")

        def unit_pst(u):
            ht = u % KH
            ucol = (u // KH) * TCH
            sl2 = u % 2
            W("vector", "sc", silu_sc[u])
            WT("vector", pst_tok[0])
            pv = I("vector", lambda e, bank=pb[u % 4], sl2=sl2: e.tensor_tensor(
                pst2[:], sgb2[sl2][:], bank[:, TCH:2 * TCH], ALU.mult), "ve")
            bank_ve[u % 4] = pv
            bank_sc[u % 4] = silu_sc[u]
            sgb_ve[u % 2] = pv
            W("gpsimd", "ve", pv)
            pst_tok[0] = ST(lambda e, ht=ht, w4=ucol // 512, coff=ucol % 512,
                            sl2=sl2: e.dma_start(
                out=psp[:, w4, ht, coff:coff + TCH], in_=pst2[:]), 6 + sl2)

        for cI in range(NCH):
            col0 = cI * TCH
            w4 = col0 // 512
            coff = col0 % 512
            W("sync", "ve", ev_xpre_ve)
            W("sync", "gp", ev_xpre_gp)
            W("sync", "te", qwin_te[0])
            if cI == 0:
                for tok in snap_qsp:
                    WT("sync", tok)
            t_qw = [LD(lambda e, col0=col0, h2=h2: e.dma_start(
                out=qwin[:, h2 * 8:(h2 + 1) * 8, :],
                in_=qsp[:, h2 * 8:(h2 + 1) * 8, col0:col0 + TCH]), 7)
                for h2 in range(2)]
            wgu_load(0, cI)
            wgu_load(1, cI)
            if cI == 0:
                for tok in snap_p2:
                    WT("sync", tok)
                W("vector", "te", te_p2)
                W("vector", "sc", sc_p2c)
                for tok in snap_qsp:
                    WT("vector", tok)
                WT("vector", *wgu_tok[KH - 1])
                ve_z0 = I("vector", lambda e: e.memset(
                    xpre[:].rearrange("p a b -> p (a b)"), 0.0), "ve")
                W("sync", "ve", ve_z0)
                for q4 in range(4):
                    t_xpre = LD(lambda e, q4=q4: e.dma_start(
                        out=xpre[:, q4 * 8:(q4 + 1) * 8, 3:259],
                        in_=xsp[:, q4 * 8:(q4 + 1) * 8, 0:TCH]), 5)
            else:
                for q4 in range(4):
                    t_xpre = LD(lambda e, col0=col0, q4=q4: e.dma_start(
                        out=xpre[:, q4 * 8:(q4 + 1) * 8, :],
                        in_=xsp[:, q4 * 8:(q4 + 1) * 8, col0 - 3: col0 + TCH]), 5)
            W("sync", "ve", ev_zread_ve)
            for q4 in range(4):
                t_z = LD(lambda e, col0=col0, q4=q4: e.dma_start(
                    out=zb[:, q4 * 8:(q4 + 1) * 8, :],
                    in_=zsp[:, q4 * 8:(q4 + 1) * 8, col0:col0 + TCH]), 6)
            t_z_last = t_z

            WT("vector", t_xpre)
            I("vector", lambda e: e.tensor_tensor(
                xc_[:], xpre[:, :, 0:TCH],
                convw_sb[:, :, 0:1].broadcast_to([128, KDI, TCH]), ALU.mult), "ve")
            for j in (1, 2, 3):
                for hh in range(4):
                    I("vector", lambda e, j=j, hh=hh: e.tensor_tensor(
                        cq_t[:], xpre[:, hh * 8:(hh + 1) * 8, j:j + TCH],
                        convw_sb[:, hh * 8:(hh + 1) * 8, j:j + 1].broadcast_to([128, 8, TCH]),
                        ALU.mult), "ve")
                    I("vector", lambda e, hh=hh: e.tensor_tensor(
                        xc_[:, hh * 8:(hh + 1) * 8, :],
                        xc_[:, hh * 8:(hh + 1) * 8, :], cq_t[:], ALU.add), "ve")
            I("vector", lambda e: e.tensor_tensor(
                xc_[:], xc_[:],
                convb_sb[:].unsqueeze(2).broadcast_to([128, KDI, TCH]), ALU.add), "ve")
            ve_conv = C["ve"]
            W("scalar", "ve", ve_conv)
            sc_silu = I("scalar", lambda e: e.activation(xcf, xcf, AF.Silu), "sc")
            WT("scalar", t_z_last)
            W("scalar", "ve", ev_zread_ve)
            sc_zsilu = I("scalar", lambda e: e.activation(
                zb[:].rearrange("p a b -> p (a b)"),
                zb[:].rearrange("p a b -> p (a b)"), AF.Silu), "sc")
            W("vector", "sc", sc_silu)
            ve_xcb = I("vector", lambda e: e.tensor_copy(xcbf, xcf), "ve")

            p_xd = pb[4][:, 0:TCH]
            p_bc = pb[4][0:32, TCH:2 * TCH]
            W("tensor", "ve", ve_xcb)
            for k in range(KDI):
                I("tensor", lambda e, k=k: e.matmul(
                    p_xd, wx_bf[:, k, 0:128], xcb[:, k, :],
                    start=(k == 0), stop=(k == KDI - 1)), "te")
            for k in range(KDI):
                I("tensor", lambda e, k=k: e.matmul(
                    p_bc, wx_bf[:, k, 128:160], xcb[:, k, :],
                    start=(k == 0), stop=(k == KDI - 1)), "te")
            te_xp = C["te"]
            W("scalar", "te", te_xp)
            I("scalar", lambda e: e.activation(dtlo[:], p_xd, AF.Copy), "sc")
            sc_xp = I("scalar", lambda e: e.activation(bc_sb[:], p_bc, AF.Copy), "sc")

            wave_sc = sc_xp
            slots = [pb[4 + s // 2][:, (s % 2) * TCH:(s % 2 + 1) * TCH]
                     for s in range(8)]
            for w in range(8):
                W("tensor", "sc", wave_sc)
                te_w = 0
                for s in range(8):
                    j = w * 8 + s
                    if j < KDI:
                        te_w = I("tensor", lambda e, j=j, s=s: e.matmul(
                            slots[s], wdt_bf[:, j * 128:(j + 1) * 128], dtlo[:],
                            start=True, stop=True), "te")
                    else:
                        n = j - KDI
                        te_w = I("tensor", lambda e, n=n, s=s: e.matmul(
                            slots[s], oneh[:, n * 128:(n + 1) * 128], bc_sb[:],
                            start=True, stop=True), "te")
                W("scalar", "te", te_w)
                for s in range(8):
                    j = w * 8 + s
                    if j < KDI:
                        wave_sc = I("scalar", lambda e, j=j, s=s: e.activation(
                            delta[:, j, :], slots[s], AF.Identity,
                            bias=dtb_sb[:, j:j + 1]), "sc")
                    else:
                        n = j - KDI
                        dst = b_all if n < 16 else c_all
                        dn = n if n < 16 else n - 16
                        wave_sc = I("scalar", lambda e, dst=dst, dn=dn, s=s: e.activation(
                            dst[:, dn, :], slots[s], AF.Copy), "sc")

            ve_sp_add = 0
            for hh in range(4):
                dsl = deltaf.rearrange("p (a b) -> p a b", b=2048)[:, hh, :]
                W("scalar", "ve", ve_sp_add)
                I("scalar", lambda e, dsl=dsl: e.activation(spq_t[:], dsl, AF.Abs), "sc")
                I("scalar", lambda e: e.activation(spq_t[:], spq_t[:], AF.Exp, scale=-1.0), "sc")
                I("scalar", lambda e: e.activation(spq_t[:], spq_t[:], AF.Ln, bias=1.0), "sc")
                sc_sp = I("scalar", lambda e, dsl=dsl: e.activation(dsl, dsl, AF.Relu), "sc")
                W("vector", "sc", sc_sp)
                ve_sp_add = I("vector", lambda e, dsl=dsl: e.tensor_tensor(dsl, dsl, spq_t[:], ALU.add), "ve")
            ve_pre = C["ve"]

            fin_state = {"yg_ve": 0, "t_yg8": t_yg8[0]}

            def emit_dxdt_dbx(dt):
                I("vector", lambda e, dt=dt: e.tensor_tensor(
                    dxdtb[:], delta[:, dt, :], xc_[:, dt, :], ALU.mult), "ve")
                return I("vector", lambda e: e.tensor_tensor(
                    dbx_b[:], dxdtb[:].unsqueeze(1).broadcast_to([128, DS, TCH]),
                    b_all[:], ALU.mult), "ve")

            def emit_exps(dt, ve_dbx):
                for n in range(DS):
                    W("scalar", "ve", ve_dbx)
                    exp_sc[n] = I("scalar", lambda e, dt=dt, n=n: e.activation(
                        dAs_t[n][:], delta[:, dt, :], AF.Exp,
                        scale=a_sb[:, dt, n:n + 1]), "sc")

            ve_dbx = emit_dxdt_dbx(0)
            emit_exps(0, ve_dbx)
            for dt in range(KDI):
                g = dt % 8
                unit_mm(cI * KH + 2 * dt)
                unit_mm(cI * KH + 2 * dt + 1)
                for n in range(DS):
                    W("vector", "sc", exp_sc[n])
                    I("vector", lambda e, n=n, dt=dt: e.tensor_tensor_scan(
                        hall[:, n, :], dAs_t[n][:], dbx_b[:, n, :], hcar[:, dt, n:n + 1],
                        ALU.mult, ALU.add), "ve", nodep=(n > 0))
                sl_ve = C["ve"]
                I("vector", lambda e: e.tensor_tensor(ct[:], hall[:], c_all[:], ALU.mult), "ve")
                I("vector", lambda e: e.tensor_tensor(
                    ct[:, 0:8, :], ct[:, 0:8, :], ct[:, 8:16, :], ALU.add), "ve")
                I("vector", lambda e: e.tensor_tensor(
                    ct[:, 0:4, :], ct[:, 0:4, :], ct[:, 4:8, :], ALU.add), "ve")
                I("vector", lambda e: e.tensor_tensor(
                    ct[:, 0:2, :], ct[:, 0:2, :], ct[:, 2:4, :], ALU.add), "ve")
                I("vector", lambda e: e.tensor_tensor(
                    ydtb[:], ct[:, 0, :], ct[:, 1, :], ALU.add), "ve")
                I("vector", lambda e, dt=dt: e.scalar_tensor_tensor(
                    ydtb[:], xc_[:, dt, :], d_sb[:, dt:dt + 1], ydtb[:],
                    ALU.mult, ALU.add), "ve")
                if dt == 0:
                    W("vector", "sc", sc_zsilu)
                if g == 0:
                    WT("vector", fin_state["t_yg8"])
                fin_state["yg_ve"] = I("vector", lambda e, g=g, dt=dt: e.tensor_tensor(
                    yg8[:, g, :], ydtb[:], zb[:, dt, :], ALU.mult), "ve")
                if g == 7:
                    gi = dt // 8
                    ve_g = C["ve"]
                    W("gpsimd", "ve", ve_g)
                    fin_state["t_yg8"] = ST(lambda e, w4=w4, coff=coff, gi=gi: e.dma_start(
                        out=ygsp[:, w4, gi * 8:(gi + 1) * 8, coff:coff + TCH],
                        in_=yg8[:]), 5)
                if dt + 1 < KDI:
                    ve_dbx = emit_dxdt_dbx(dt + 1)
                if dt > 0:
                    unit_pst(cI * KH + 2 * dt - 2)
                    unit_pst(cI * KH + 2 * dt - 1)
                W("scalar", "ve", sl_ve)
                I("scalar", lambda e, dt=dt: e.activation(
                    hcar[:, dt, :], hall[:, :, TCH - 1], AF.Copy), "sc")
                if dt + 1 < KDI:
                    emit_exps(dt + 1, ve_dbx)
                unit_silu(cI * KH + 2 * dt)
                unit_silu(cI * KH + 2 * dt + 1)
            unit_pst(cI * KH + 62)
            unit_pst(cI * KH + 63)
            t_yg8[0] = fin_state["t_yg8"]
            ev_zread_ve = fin_state["yg_ve"]
            ev_xpre_ve = C["ve"]
            ev_xpre_gp = C["gp"]
        snap_p3 = st_snap()
        ve_p3 = C["ve"]
        sc_p3 = C["sc"]
        gp_p3 = C["gp"]

        snap_p5 = snap_p3
        ve_p5 = ve_p3
        sc_p5 = sc_p3
        MARKS["P5"] = {k: len(v) for k, v in streams.items()}

        wdb = vb(0, KH, 512)
        pwinH = [vb(16384, 32, 512), vb(24576, 32, 512)]
        wd_big = [vf(32768, 4, 512), vf(34816, 4, 512)]
        wdT_t = wdT.rearrange("(k p) l -> p k l", p=128)
        ostf = vf(36864, 2, 512)
        t_ost2 = [None, None]
        te_p6a_grp = 0
        te_p3c = C["te"]
        ev_bank6 = [sc_p3] * 4
        ev_sgn = [0, 0]
        W("tensor", "ve", ve_p3)
        W("sync", "ve", ve_p5)
        W("sync", "sc", sc_p5)
        W("sync", "te", te_p3c)
        for tok in snap_p3:
            WT("sync", tok)
        pw_tok = [LD(lambda e, h=h: e.dma_start(
            out=pwinH[h][:], in_=psp[:, 0, h * 32:(h + 1) * 32, :]), 12 + h)
            for h in range(2)]
        pw_te = [0, 0]
        for mg in range(4):
            for q in range(16):
                sl = q % 2
                if mg == 0 and q == 0:
                    W("sync", "ve", ve_p5)
                    W("sync", "sc", sc_p5)
                    for tok in snap_p3:
                        WT("sync", tok)
                W("sync", "sc", ev_sgn[sl])
                t_wd = LD(lambda e, q=q, mg=mg, sl=sl: e.dma_start(
                    out=wd_big[sl][:],
                    in_=wdT_t[:, q * 4:(q + 1) * 4, mg * 512:(mg + 1) * 512]), 10 + sl)
                WT("scalar", t_wd)
                if mg == 0 and q == 0:
                    W("scalar", "ve", ve_p5)
                if q == 0:
                    W("scalar", "te", te_p6a_grp)
                ev_sgn[sl] = I("scalar", lambda e, q=q, sl=sl: e.activation(
                    wdb[:, q * 4:(q + 1) * 4, :].rearrange("p a b -> p (a b)"),
                    wd_big[sl][:].rearrange("p a b -> p (a b)"), AF.Sign), "sc")
            sc_wd = C["sc"]
            for wdw in range(4):
                widx = mg * 4 + wdw
                nw = (widx + 1) % 4
                W("tensor", "sc", sc_wd)
                WT("tensor", pw_tok[0])
                for ms in range(4):
                    bank = (ms + wdw) % 4
                    W("tensor", "sc", ev_bank6[bank])
                    for k in range(32):
                        I("tensor", lambda e, k=k, ms=ms, bank=bank: e.matmul(
                            pb[bank][:], wdb[:, k, ms * 128:(ms + 1) * 128],
                            pwinH[0][:, k, :],
                            start=(k == 0), stop=False), "te")
                pw_te[0] = C["te"]
                if widx + 1 < 16:
                    W("sync", "te", pw_te[0])
                    pw_tok[0] = LD(lambda e, nw=nw: e.dma_start(
                        out=pwinH[0][:], in_=psp[:, nw, 0:32, :]), 12)
                WT("tensor", pw_tok[1])
                for ms in range(4):
                    bank = (ms + wdw) % 4
                    for k in range(32):
                        I("tensor", lambda e, k=k, ms=ms, bank=bank: e.matmul(
                            pb[bank][:], wdb[:, 32 + k, ms * 128:(ms + 1) * 128],
                            pwinH[1][:, k, :],
                            start=False, stop=(k == 31)), "te")
                    te_mm = C["te"]
                    W("scalar", "te", te_mm)
                    osl = (ms + 4 * wdw) % 2
                    WT("scalar", t_ost2[osl])
                    sc_ost = I("scalar", lambda e, bank=bank, osl=osl: e.activation(
                        ostf[:, osl, :], pb[bank][:], AF.Copy), "sc")
                    ev_bank6[bank] = sc_ost
                    W("gpsimd", "sc", sc_ost)
                    t_ost2[osl] = ST(lambda e, mg=mg, ms=ms, w=wdw, osl=osl: e.dma_start(
                        out=offn[:, mg * 4 + ms, w * 512:(w + 1) * 512], in_=ostf[:, osl, :]), 1 + 3 * osl)
                pw_te[1] = C["te"]
                if widx + 1 < 16:
                    W("sync", "te", pw_te[1])
                    pw_tok[1] = LD(lambda e, nw=nw: e.dma_start(
                        out=pwinH[1][:], in_=psp[:, nw, 32:64, :]), 13)
                te_p6a_grp = C["te"]
        snap_p6a = st_snap()
        sc_p6a = C["sc"]
        te_p6a = C["te"]

        wob = vb(0, KDI, 512)
        ygwin = vb(8192, KDI, 512)
        wo_big = [vf(16384, 8, 512), vf(20480, 8, 512)]
        xct2 = [vf(24576, 512), vf(25088, 512)]
        oft2 = [vf(25600, 512), vf(26112, 512)]
        ostg2 = [vf(26624, 512), vf(27136, 512)]
        woT_t = woT.rearrange("(k p) l -> p k l", p=128)
        xcol_t = xcol.rearrange("(t p) l -> p t l", p=128)
        ev_bank_cmb = [0, 0, 0, 0]
        ev_cmb_slot = [0, 0]
        t_out2 = [None, None]
        t_xct2 = [None, None]
        t_oft2 = [None, None]
        te_p6b_grp = 0
        ev_cst = [0, 0]
        for mg in range(4):
            for q in range(4):
                sl = q % 2
                if mg == 0 and q == 0:
                    W("sync", "sc", sc_p6a)
                W("sync", "sc", ev_cst[sl])
                t_wo = LD(lambda e, q=q, mg=mg, sl=sl: e.dma_start(
                    out=wo_big[sl][:],
                    in_=woT_t[:, q * 8:(q + 1) * 8, mg * 512:(mg + 1) * 512]), 10 + sl)
                WT("scalar", t_wo)
                if mg == 0 and q == 0:
                    W("scalar", "sc", sc_p6a)
                if q == 0:
                    W("scalar", "te", te_p6b_grp)
                ev_cst[sl] = I("scalar", lambda e, q=q, sl=sl: e.activation(
                    wob[:, q * 8:(q + 1) * 8, :].rearrange("p a b -> p (a b)"),
                    wo_big[sl][:].rearrange("p a b -> p (a b)"), AF.Copy), "sc")
            sc_wo = C["sc"]
            for wdw in range(4):
                if mg == 0 and wdw == 0:
                    for tok in snap_p3:
                        WT("sync", tok)
                    W("sync", "sc", sc_p6a)
                W("sync", "te", max(te_p6a if (mg == 0 and wdw == 0) else 0, te_p6b_grp))
                t_yg = LD(lambda e, w=wdw: e.dma_start(
                    out=ygwin[:], in_=ygsp[:, w, :, :]), 12)
                for ms in range(4):
                    bank = (ms + wdw) % 4
                    gidx = ms + 4 * wdw
                    osl = gidx % 2
                    row = mg * 4 + ms
                    W("tensor", "sc", sc_wo)
                    WT("tensor", t_yg)
                    W("tensor", "ve", ev_bank_cmb[bank])
                    for k in range(KDI):
                        I("tensor", lambda e, k=k, ms=ms, bank=bank: e.matmul(
                            pb[bank][:], wob[:, k, ms * 128:(ms + 1) * 128], ygwin[:, k, :],
                            start=(k == 0), stop=(k == KDI - 1)), "te")
                    te_mm = C["te"]
                    W("sync", "ve", ev_cmb_slot[osl])
                    if mg == 0 and wdw == 0 and ms == 0:
                        for tok in snap_p6a:
                            WT("sync", tok)
                    t_xct2[osl] = LD(lambda e, row=row, w=wdw, osl=osl: e.dma_start(
                        out=xct2[osl], in_=xcol_t[:, row, w * 512:(w + 1) * 512]), 5 + 2 * osl)
                    t_oft2[osl] = LD(lambda e, row=row, w=wdw, osl=osl: e.dma_start(
                        out=oft2[osl], in_=offn[:, row, w * 512:(w + 1) * 512]), 6 + 8 * osl)
                    W("vector", "te", te_mm)
                    WT("vector", t_xct2[osl], t_oft2[osl])
                    WT("vector", t_out2[osl])
                    I("vector", lambda e, bank=bank, osl=osl: e.tensor_tensor(
                        ostg2[osl], pb[bank][:], xct2[osl], ALU.add), "ve")
                    ev_cmb_slot[osl] = I("vector", lambda e, osl=osl: e.tensor_tensor(
                        ostg2[osl], ostg2[osl], oft2[osl], ALU.add), "ve")
                    ev_bank_cmb[bank] = ev_cmb_slot[osl]
                    W("gpsimd", "ve", ev_cmb_slot[osl])
                    t_out2[osl] = ST(lambda e, row=row, w=wdw, osl=osl: e.dma_start(
                        out=outT[:, row, w * 512:(w + 1) * 512], in_=ostg2[osl]), 2 + osl)
                te_p6b_grp = C["te"]
        for tok in st_snap():
            WT("gpsimd", tok)
        if CUT is not None:
            cut = MARKS[CUT]
            for k in streams:
                streams[k] = streams[k][:cut[k]]

        @block.sync
        def _(eng):
            for f in streams["sync"]:
                f(eng)

        @block.tensor
        def _(eng):
            for f in streams["tensor"]:
                f(eng)

        @block.vector
        def _(eng):
            for f in streams["vector"]:
                f(eng)

        @block.scalar
        def _(eng):
            for f in streams["scalar"]:
                f(eng)

        @block.gpsimd
        def _(eng):
            for f in streams["gpsimd"]:
                f(eng)

    return nc


def kernel(L0_attn_hidden_states, L0_attn_in_proj_w, L0_attn_conv_w,
           L0_attn_conv_b, L0_attn_x_proj_w, L0_attn_dt_proj_w,
           L0_attn_dt_proj_b, L0_attn_A_log, L0_attn_D, L0_attn_out_proj_w,
           L0_ffn_W_g, L0_ffn_W_u, L0_ffn_W_d):
    x = np.ascontiguousarray(np.asarray(L0_attn_hidden_states, dtype=np.float32))
    B = x.shape[0]

    def t(a):
        return np.ascontiguousarray(np.asarray(a, dtype=np.float32).T)

    def perpart(a):
        a = np.asarray(a, dtype=np.float32)
        sh = a.shape[1:]
        return np.ascontiguousarray(
            a.reshape(KDI, 128, *sh).transpose(1, 0, *range(2, 2 + len(sh))))

    winT = t(L0_attn_in_proj_w)
    wxT = np.ascontiguousarray(
        np.asarray(L0_attn_x_proj_w, np.float32).T.reshape(KDI, 128, 160).transpose(1, 0, 2))
    wdtT = t(L0_attn_dt_proj_w)
    woT = t(L0_attn_out_proj_w)
    wgT = t(L0_ffn_W_g)
    wuT = t(L0_ffn_W_u)
    wdT = t(L0_ffn_W_d)
    convw_h = perpart(L0_attn_conv_w)
    convb_h = perpart(L0_attn_conv_b)
    dtb_h = perpart(L0_attn_dt_proj_b)
    alog_h = perpart(L0_attn_A_log)
    dvec_h = perpart(L0_attn_D)

    if "nc" not in _CACHE:
        _CACHE["nc"] = build_program()
    nc = _CACHE["nc"]

    shared = dict(winT=winT, wxT=wxT, wdtT=wdtT, woT=woT, wgT=wgT, wuT=wuT,
                  wdT=wdT, convw=convw_h, convb=convb_h, dtb=dtb_h,
                  alog=alog_h, dvec=dvec_h)
    in_maps = []
    for b in range(B):
        m = dict(shared)
        m["xrow"] = np.ascontiguousarray(x[b])
        m["xcol"] = np.ascontiguousarray(x[b].T)
        in_maps.append(m)

    res = run_bass_kernel_spmd(nc, in_maps, core_ids=list(range(B)))
    outs = []
    for b in range(B):
        arr = np.asarray(res.results[b]["outT"])
        outs.append(arr.transpose(2, 1, 0).reshape(L, D))
    return np.stack(outs).astype(np.float32)


# version 19
# speedup vs baseline: 1.0517x; 1.0194x over previous
import contextlib

import numpy as np

import concourse.bass as bass
import concourse.mybir as mybir
from concourse.bass_utils import run_bass_kernel_spmd

F32 = mybir.dt.float32
BF16 = mybir.dt.bfloat16
AF = mybir.ActivationFunctionType
ALU = mybir.AluOpType
AX = mybir.AxisListType

L = 2048
D = 2048
DI = 4096
DS = 16
H = 8192
TCH = 256
NCH = L // TCH
NT = 16
KD = D // 128
KDI = DI // 128
KH = H // 128
NLD = 17
NST = 8

_CACHE = {}
CUT = None


def build_program():
    nc = bass.Bass("TRN2", target_bir_lowering=False)

    xrow = nc.dram_tensor("xrow", [L, D], F32, kind="ExternalInput").ap()
    xcol = nc.dram_tensor("xcol", [D, L], F32, kind="ExternalInput").ap()
    winT = nc.dram_tensor("winT", [D, 2 * DI], F32, kind="ExternalInput").ap()
    wxT = nc.dram_tensor("wxT", [128, KDI, 160], F32, kind="ExternalInput").ap()
    wdtT = nc.dram_tensor("wdtT", [128, DI], F32, kind="ExternalInput").ap()
    woT = nc.dram_tensor("woT", [DI, D], F32, kind="ExternalInput").ap()
    wgT = nc.dram_tensor("wgT", [D, H], F32, kind="ExternalInput").ap()
    wuT = nc.dram_tensor("wuT", [D, H], F32, kind="ExternalInput").ap()
    wdT = nc.dram_tensor("wdT", [H, D], F32, kind="ExternalInput").ap()
    convw = nc.dram_tensor("convw", [128, KDI, 4], F32, kind="ExternalInput").ap()
    convb = nc.dram_tensor("convb", [128, KDI], F32, kind="ExternalInput").ap()
    dtb = nc.dram_tensor("dtb", [128, KDI], F32, kind="ExternalInput").ap()
    alog = nc.dram_tensor("alog", [128, KDI, DS], F32, kind="ExternalInput").ap()
    dvec = nc.dram_tensor("dvec", [128, KDI], F32, kind="ExternalInput").ap()

    outT = nc.dram_tensor("outT", [128, KD, L], F32, kind="ExternalOutput").ap()
    wgusp = nc.dram_tensor("wgusp", [128, KH, 4096], BF16, kind="Internal").ap()
    xsp = nc.dram_tensor("xsp", [128, KDI, L], BF16, kind="Internal").ap()
    zsp = nc.dram_tensor("zsp", [128, KDI, L], BF16, kind="Internal").ap()
    qsp = nc.dram_tensor("qsp", [128, KD, L], BF16, kind="Internal").ap()
    ygsp = nc.dram_tensor("ygsp", [128, 4, KDI, 512], BF16, kind="Internal").ap()
    psp = nc.dram_tensor("psp", [128, 4, KH, 512], BF16, kind="Internal").ap()
    offn = nc.dram_tensor("offn", [128, KD, L], F32, kind="Internal").ap()

    streams = {k: [] for k in ("sync", "tensor", "vector", "scalar", "gpsimd")}
    C = {"ve": 0, "sc": 0, "te": 0, "gp": 0}
    ld_cnt = [0] * NLD
    st_cnt = [0] * NST
    ld_rr = [0]
    st_rr = [0]

    with contextlib.ExitStack() as ctx:
        sems = {name: ctx.enter_context(nc.semaphore(name)) for name in C}
        ldsem = [ctx.enter_context(nc.semaphore(f"la{i}")) for i in range(NLD)]
        stsem = [ctx.enter_context(nc.semaphore(f"sa{i}")) for i in range(NST)]

        def W(eng, sem, val):
            if val and val > 0:
                streams[eng].append(lambda e, s=sems[sem], v=val: e.wait_ge(s, v))

        _SELF = {"vector": "ve", "scalar": "sc", "gpsimd": "gp"}

        def I(eng, builder, sem=None, amt=1, nodep=False):
            own = _SELF.get(eng)
            if nodep:
                own = None
            if own is not None and C[own] > 0 and sem is not None:
                streams[eng].append(
                    lambda e, s=sems[own], v=C[own]: e.wait_ge(s, v))
            if sem is not None:
                C[sem] += amt
                streams[eng].append(
                    lambda e, b=builder, s=sems[sem], a=amt: b(e).then_inc(s, a))
            else:
                streams[eng].append(lambda e, b=builder: b(e))
            return C[sem] if sem else None

        def LD(builder, lane):
            i = lane
            ld_cnt[i] += 16
            streams["sync"].append(
                lambda e, b=builder, s=ldsem[i]: b(e).then_inc(s, 16))
            return ("L", i, ld_cnt[i])

        def ST(builder, lane):
            i = lane
            st_cnt[i] += 16
            streams["gpsimd"].append(
                lambda e, b=builder, s=stsem[i]: b(e).then_inc(s, 16))
            return ("S", i, st_cnt[i])

        def LDG(builder, lane):
            i = lane
            ld_cnt[i] += 16
            streams["gpsimd"].append(
                lambda e, b=builder, s=ldsem[i]: b(e).then_inc(s, 16))
            return ("L", i, ld_cnt[i])

        def WT(eng, *toks):
            for tok in toks:
                if tok is None:
                    continue
                kind, i, v = tok
                s = ldsem[i] if kind == "L" else stsem[i]
                streams[eng].append(lambda e, s=s, v=v: e.wait_ge(s, v))

        def st_snap():
            return [("S", i, st_cnt[i]) for i in range(NST) if st_cnt[i]]

        PA = ctx.enter_context(nc.sbuf_tensor("PA", [128, 40960], F32))
        PAb = PA[:].bitcast(BF16)

        oneh = ctx.enter_context(nc.sbuf_tensor("oneh", [32, 32 * 128], BF16))
        wx_bf = ctx.enter_context(nc.sbuf_tensor("wx_bf", [128, KDI, 160], BF16))
        wdt_bf = ctx.enter_context(nc.sbuf_tensor("wdt_bf", [128, DI], BF16))
        a_sb = ctx.enter_context(nc.sbuf_tensor("a_sb", [128, KDI, DS], F32))
        convw_sb = ctx.enter_context(nc.sbuf_tensor("convw_sb", [128, KDI, 4], F32))
        convb_sb = ctx.enter_context(nc.sbuf_tensor("convb_sb", [128, KDI], F32))
        dtb_sb = ctx.enter_context(nc.sbuf_tensor("dtb_sb", [128, KDI], F32))
        d_sb = ctx.enter_context(nc.sbuf_tensor("d_sb", [128, KDI], F32))
        hcar = ctx.enter_context(nc.sbuf_tensor("hcar", [128, KDI, DS], F32))
        dtlo = ctx.enter_context(nc.sbuf_tensor("dtlo", [128, TCH], BF16))
        bc_sb = ctx.enter_context(nc.sbuf_tensor("bc_sb", [32, TCH], BF16))
        dAs_t = [ctx.enter_context(nc.sbuf_tensor(f"dA{i}", [128, TCH], F32))
                 for i in range(15)]
        stat = ctx.enter_context(nc.sbuf_tensor("stat", [128, 8 * NT], F32))
        identb = ctx.enter_context(nc.sbuf_tensor("identb", [128, 128], BF16))
        cst = ctx.enter_context(nc.sbuf_tensor("cst", [128, 2], F32))
        eps5 = cst[:, 0:1]
        eps8 = cst[:, 1:2]

        pb = [ctx.enter_context(nc.psum_tensor(f"pb{i}", [128, 512], F32))
              for i in range(8)]

        block = ctx.enter_context(nc.Block())

        def vf(off, *shape):
            n = int(np.prod(shape))
            ap = PA[:, off:off + n]
            if len(shape) == 2:
                ap = ap.rearrange("p (a b) -> p a b", b=shape[-1])
            elif len(shape) == 3:
                ap = ap.rearrange("p (a b c) -> p a b c", b=shape[1], c=shape[2])
            return ap

        def vb(off_f32, *shape):
            n = int(np.prod(shape))
            ap = PAb[:, 2 * off_f32: 2 * off_f32 + n]
            if len(shape) == 3:
                ap = ap.rearrange("p (a b c) -> p a b c", b=shape[1], c=shape[2])
            elif len(shape) == 2:
                ap = ap.rearrange("p (a b) -> p a b", b=shape[-1])
            return ap

        wx_stage = PA[:, 0:KDI * 160]
        wdt_stage = PA[:, 5120:5120 + DI]
        t_wx = LD(lambda e: e.dma_start(out=wx_stage, in_=wxT.rearrange("p a b -> p (a b)")), 0)
        t_wdt = LD(lambda e: e.dma_start(out=wdt_stage, in_=wdtT), 0)
        t_a = LD(lambda e: e.dma_start(out=a_sb[:], in_=alog), 0)
        t_cw = LD(lambda e: e.dma_start(out=convw_sb[:], in_=convw), 0)
        t_cb = LD(lambda e: e.dma_start(out=convb_sb[:], in_=convb), 0)
        t_db = LD(lambda e: e.dma_start(out=dtb_sb[:], in_=dtb), 0)
        t_dv = LD(lambda e: e.dma_start(out=d_sb[:], in_=dvec), 0)

        WT("vector", t_dv)
        I("vector", lambda e: e.tensor_copy(
            wx_bf[:].rearrange("p a b -> p (a b)"), wx_stage), "ve")
        I("vector", lambda e: e.tensor_copy(wdt_bf[:], wdt_stage), "ve")
        WT("scalar", t_dv)
        sc_aexp = I("scalar", lambda e: e.activation(
            a_sb[:].rearrange("p a b -> p (a b)"),
            a_sb[:].rearrange("p a b -> p (a b)"), AF.Exp), "sc")
        W("vector", "sc", sc_aexp)
        I("vector", lambda e: e.tensor_scalar(
            a_sb[:].rearrange("p a b -> p (a b)"),
            a_sb[:].rearrange("p a b -> p (a b)"), -1.0, None, ALU.mult), "ve")
        I("vector", lambda e: e.memset(cst[:, 0:1], 1e-5), "ve")
        I("vector", lambda e: e.memset(cst[:, 1:2], 1e-8), "ve")
        tmpI = PA[:, 16384:16384 + 128]
        tmpO = PA[0:32, 16512:16512 + 4096]
        gp_o = None
        I("gpsimd", lambda e: e.iota(tmpI, [[-1, 128]], base=0, channel_multiplier=1,
                                     allow_small_or_imprecise_dtypes=True), "gp")
        gp_o = I("gpsimd", lambda e: e.iota(tmpO, [[-1, 32], [0, 128]], base=0,
                                            channel_multiplier=1,
                                            allow_small_or_imprecise_dtypes=True), "gp")
        W("vector", "gp", gp_o)
        I("vector", lambda e: e.tensor_single_scalar(identb[:], tmpI, 0.0, ALU.is_equal), "ve")
        I("vector", lambda e: e.tensor_single_scalar(oneh[:], tmpO, 0.0, ALU.is_equal), "ve")
        I("vector", lambda e: e.memset(hcar[:].rearrange("p a b -> p (a b)"), 0.0), "ve")
        ve_p0 = C["ve"]
        MARKS = {}
        MARKS["P0"] = {k: len(v) for k, v in streams.items()}

        xr = [vf(0, 2048), vf(2048, 2048)]
        t0 = vf(8192, 2048)
        tq = vf(10240, 2048)
        mk = vf(12288, 2048)
        xntok = vb(14336, 2048)
        xqtok = vb(15360, 2048)
        qtt = vb(16384, NT, 128)
        xnT = vb(18432, KD, 2048)
        bnb = vf(4096, NT, 24)
        magg = vf(4480, NT, 2)
        stat2 = vf(4512, 3, NT)
        xrow_t = xrow.rearrange("(t p) d -> p t d", p=128)

        def scol(j):
            return stat[:, j * NT:(j + 1) * NT]

        ev_xrA = [0, 0]
        for tt in range(NT):
            sl = tt % 2
            W("sync", "ve", max(ev_xrA[sl], ve_p0 if tt == 0 else 0))
            t_xr = LD(lambda e, tt=tt, sl=sl: e.dma_start(out=xr[sl], in_=xrow_t[:, tt, :]), 1 + sl)
            WT("vector", t_xr)
            for bq in range(4):
                I("vector", lambda e, tt=tt, sl=sl, bq=bq: e.bn_stats(
                    bnb[:, tt, bq * 6:(bq + 1) * 6],
                    xr[sl].rearrange("p (a b) -> p a b", b=512)[:, bq, :]), "ve")
            I("vector", lambda e, tt=tt: e.bn_aggr(magg[:, tt, :], bnb[:, tt, :]), "ve")
            I("vector", lambda e, tt=tt, sl=sl: e.reduce_max(
                stat2[:, 0, tt:tt + 1], xr[sl], AX.X), "ve")
            ev_xrA[sl] = I("vector", lambda e, tt=tt, sl=sl: e.tensor_reduce(
                stat2[:, 1, tt:tt + 1], xr[sl], AX.X, ALU.min), "ve")

        m_s = magg[:, :, 0]
        v_s = magg[:, :, 1]
        r_s, r2_s, rs_s, c2_s, nmr_s, b2_s = (scol(2), scol(3), scol(4),
                                              scol(5), scol(6), scol(7))
        mx_s, mn_s, rs128_s = stat2[:, 0, :], stat2[:, 1, :], stat2[:, 2, :]
        t6_s, t7_s = scol(0), scol(1)
        ve_A = C["ve"]
        W("scalar", "ve", ve_A)
        I("scalar", lambda e: e.activation(t6_s, v_s, AF.Ln, bias=eps5), "sc")
        sc_r = I("scalar", lambda e: e.activation(r_s, t6_s, AF.Exp, scale=-0.5), "sc")
        W("vector", "sc", sc_r)
        I("vector", lambda e: e.tensor_tensor(t6_s, v_s, r_s, ALU.mult), "ve")
        ve_v2 = I("vector", lambda e: e.tensor_tensor(t6_s, t6_s, r_s, ALU.mult), "ve")
        W("scalar", "ve", ve_v2)
        I("scalar", lambda e: e.activation(t6_s, t6_s, AF.Ln, bias=eps8), "sc")
        sc_r2 = I("scalar", lambda e: e.activation(r2_s, t6_s, AF.Exp, scale=-0.5), "sc")
        W("vector", "sc", sc_r2)
        I("vector", lambda e: e.tensor_tensor(t6_s, mx_s, m_s, ALU.subtract), "ve")
        I("vector", lambda e: e.tensor_tensor(t7_s, m_s, mn_s, ALU.subtract), "ve")
        I("vector", lambda e: e.tensor_tensor(t6_s, t6_s, t7_s, ALU.max), "ve")
        I("vector", lambda e: e.tensor_tensor(t6_s, t6_s, r_s, ALU.mult), "ve")
        I("vector", lambda e: e.tensor_tensor(t6_s, t6_s, r2_s, ALU.mult), "ve")
        I("vector", lambda e: e.reciprocal(t7_s, t6_s), "ve")
        I("vector", lambda e: e.tensor_scalar(rs_s, t6_s, 1.0 / 127.0, None, ALU.mult), "ve")
        I("vector", lambda e: e.tensor_scalar(rs128_s, rs_s, 128.0, None, ALU.mult), "ve")
        I("vector", lambda e: e.tensor_scalar(t7_s, t7_s, 127.0, None, ALU.mult), "ve")
        I("vector", lambda e: e.tensor_tensor(c2_s, r_s, r2_s, ALU.mult), "ve")
        I("vector", lambda e: e.tensor_tensor(c2_s, c2_s, t7_s, ALU.mult), "ve")
        I("vector", lambda e: e.tensor_tensor(nmr_s, m_s, r_s, ALU.mult), "ve")
        I("vector", lambda e: e.tensor_scalar(nmr_s, nmr_s, -1.0, None, ALU.mult), "ve")
        I("vector", lambda e: e.tensor_tensor(b2_s, m_s, c2_s, ALU.mult), "ve")
        ve_stats = I("vector", lambda e: e.tensor_scalar(
            b2_s, b2_s, -1.0, 128.5, ALU.mult, ALU.add), "ve")

        ev_xr_sc = [0, 0]
        ev_tq = [0] * 8
        ev_tok_te = 0
        t_qtt = None
        for tt in range(NT):
            sl = tt % 2
            W("sync", "sc", ev_xr_sc[sl])
            W("sync", "ve", ve_stats if tt < 2 else 0)
            t_xr = LD(lambda e, tt=tt, sl=sl: e.dma_start(out=xr[sl], in_=xrow_t[:, tt, :]), 1 + sl)
            WT("scalar", t_xr)
            if tt == 0:
                W("scalar", "ve", ve_stats)
            W("scalar", "te", ev_tok_te)
            I("scalar", lambda e, sl=sl, tt=tt: e.activation(
                xntok, xr[sl], AF.Identity, scale=r_s[:, tt:tt + 1],
                bias=nmr_s[:, tt:tt + 1]), "sc")
            sc_t0 = I("scalar", lambda e, sl=sl, tt=tt: e.activation(
                t0, xr[sl], AF.Identity, scale=c2_s[:, tt:tt + 1],
                bias=b2_s[:, tt:tt + 1]), "sc")
            ev_xr_sc[sl] = sc_t0
            W("vector", "sc", sc_t0)
            I("vector", lambda e: e.tensor_copy(tq, t0), "ve")
            for cbit in (128.0, 64.0, 32.0, 16.0, 8.0, 4.0, 2.0, 1.0):
                I("vector", lambda e, c=cbit: e.tensor_single_scalar(mk, tq, c, ALU.is_ge), "ve")
                I("vector", lambda e, c=cbit: e.scalar_tensor_tensor(
                    tq, mk, -c, tq, ALU.mult, ALU.add), "ve")
            I("vector", lambda e: e.tensor_tensor(t0, t0, tq, ALU.subtract), "ve")
            ve_xq = I("vector", lambda e, tt=tt: e.tensor_scalar(
                xqtok, t0, rs_s[:, tt:tt + 1], rs128_s[:, tt:tt + 1],
                ALU.mult, ALU.subtract), "ve")
            for j in range(2 * KD):
                qn = j % 8
                bank_b = pb[qn][:].bitcast(BF16)
                quarter = bank_b[:, 0:128]
                W("tensor", "ve", ve_xq)
                W("tensor", "sc", ev_tq[qn])
                src_ = xntok if j < KD else xqtok
                dcol = j % KD
                te_t = I("tensor", lambda e, q=quarter, s=src_, dcol=dcol: e.transpose(
                    q, s.rearrange("p (a b) -> p a b", b=128)[:, dcol, :], identb[:]), "te")
                W("scalar", "te", te_t)
                if j == KD and tt > 0:
                    WT("scalar", t_qtt)
                if j < KD:
                    ev_tq[qn] = I("scalar", lambda e, q=quarter, dcol=dcol, tt=tt: e.activation(
                        xnT[:, dcol, tt * 128:(tt + 1) * 128], q, AF.Copy), "sc")
                else:
                    ev_tq[qn] = I("scalar", lambda e, q=quarter, dcol=dcol: e.activation(
                        qtt[:, dcol, :], q, AF.Copy), "sc")
            ev_tok_te = C["te"]
            sc_tile = C["sc"]
            W("gpsimd", "sc", sc_tile)
            for h2 in range(2):
                t_qtt = ST(lambda e, tt=tt, h2=h2: e.dma_start(
                    out=qsp[:, h2 * 8:(h2 + 1) * 8, tt * 128:(tt + 1) * 128],
                    in_=qtt[:, h2 * 8:(h2 + 1) * 8, :]), 0)
        ve_p1 = C["ve"]
        sc_p1 = C["sc"]
        snap_qsp = st_snap()

        wstage2 = [vf(0, 16, 512), vf(8192, 16, 512)]
        wbf = vb(34816, 16, 512)
        xzst = vb(38912, 2, 2048)
        winT_t = winT.rearrange("(k p) (g c) -> p k g c", p=128, c=512)

        wgT_t = wgT.rearrange("(k p) (h c) -> p k h c", p=128, c=128)
        wuT_t = wuT.rearrange("(k p) (h c) -> p k h c", p=128, c=128)
        cslot_f = [vf(16384 + 512 * i, 4, 128) for i in range(4)]
        cslot_b = [vb(16384 + 512 * i, 512) for i in range(4)]
        cst_tok = [None] * 512
        cld_tok = [None] * 512
        wgu_tok = [None] * KH
        csc = [0, 0, 0, 0]

        def cast_piece_load(p):
            ht, wu, h = p // 8, (p // 4) % 2, p % 4
            sl = p % 4
            if p >= 4:
                WT("sync", cst_tok[p - 4])
            src = wgT_t if wu == 0 else wuT_t
            cld_tok[p] = LD(lambda e, ht=ht, h=h, src=src, sl=sl: e.dma_start(
                out=cslot_f[sl][:], in_=src[:, h * 4:(h + 1) * 4, ht, :]), 8 + sl)

        def cast_piece_do(p):
            ht, wu, h = p // 8, (p // 4) % 2, p % 4
            sl = p % 4
            WT("scalar", cld_tok[p])
            csc[sl] = I("scalar", lambda e, sl=sl: e.activation(
                cslot_b[sl][:], cslot_f[sl][:].rearrange("p a b -> p (a b)"),
                AF.Sign), "sc")
            W("gpsimd", "sc", csc[sl])
            if p >= 2:
                WT("gpsimd", cst_tok[p - 2])
            cst_tok[p] = ST(lambda e, ht=ht, wu=wu, h=h, sl=sl: e.dma_start(
                out=wgusp[:, ht, (wu * 4 + h) * 512:(wu * 4 + h + 1) * 512],
                in_=cslot_b[sl][:]), 3 + p % 2)
            if p % 8 == 7:
                wgu_tok[ht] = (cst_tok[p - 1], cst_tok[p])
        ev_wst_ve = [ve_p1, ve_p1]
        ev_wbf_te = 0
        ev_bank_sc = [ve_p1] * 4
        W("tensor", "sc", sc_p1)
        t_row = [None, None]
        tok_w = [None, None]
        W("sync", "ve", ve_p1)
        W("sync", "sc", sc_p1)
        for mg in range(2):
            wsl = mg % 2
            t_w0 = LD(lambda e, mg=mg, wsl=wsl: e.dma_start(out=wstage2[wsl][:, 0:8, :], in_=winT_t[:, 0:8, mg, :]), 3 + 10 * wsl)
            t_w1 = LD(lambda e, mg=mg, wsl=wsl: e.dma_start(out=wstage2[wsl][:, 8:16, :], in_=winT_t[:, 8:16, mg, :]), 4 + 10 * wsl)
            tok_w[wsl] = (t_w0, t_w1)
        for tok in snap_qsp:
            WT("sync", tok)
        for p0 in range(4):
            cast_piece_load(p0)
        for mg in range(16):
            wsl = mg % 2
            WT("vector", *tok_w[wsl])
            W("vector", "te", ev_wbf_te)
            if mg == 0:
                for tok in snap_qsp:
                    WT("vector", tok)
            ev_cast = I("vector", lambda e, wsl=wsl: e.tensor_copy(
                wbf[:].rearrange("p a b -> p (a b)"), wstage2[wsl][:].rearrange("p a b -> p (a b)")), "ve")
            ev_wst_ve[wsl] = ev_cast
            if mg + 2 < 16:
                W("sync", "ve", ev_cast)
                t_w0 = LD(lambda e, mg=mg, wsl=wsl: e.dma_start(out=wstage2[wsl][:, 0:8, :], in_=winT_t[:, 0:8, mg + 2, :]), 3 + 10 * wsl)
                t_w1 = LD(lambda e, mg=mg, wsl=wsl: e.dma_start(out=wstage2[wsl][:, 8:16, :], in_=winT_t[:, 8:16, mg + 2, :]), 4 + 10 * wsl)
                tok_w[wsl] = (t_w0, t_w1)
            for ms in range(4):
                row = mg * 4 + ms
                xsl = ms % 2
                for wdw in range(4):
                    bank = (ms * 4 + wdw) % 4
                    W("tensor", "ve", ev_cast)
                    W("tensor", "ve", ev_bank_sc[bank])
                    for k in range(16):
                        I("tensor", lambda e, k=k, ms=ms, w=wdw, bank=bank: e.matmul(
                            pb[bank][:], wbf[:, k, ms * 128:(ms + 1) * 128],
                            xnT[:, k, w * 512:(w + 1) * 512],
                            start=(k == 0), stop=(k == 15)), "te")
                    te_mm = C["te"]
                    W("vector", "te", te_mm)
                    if wdw == 0:
                        WT("vector", t_row[xsl])
                    ev_bank_sc[bank] = I("vector", lambda e, xsl=xsl, w=wdw, bank=bank: e.tensor_copy(
                        xzst[:, xsl, w * 512:(w + 1) * 512], pb[bank][:]), "ve")
                sc_row = C["ve"]
                W("gpsimd", "ve", sc_row)
                dst = xsp if row < KDI else zsp
                drow = row if row < KDI else row - KDI
                t_row[xsl] = ST(lambda e, dst=dst, drow=drow, xsl=xsl: e.dma_start(
                    out=dst[:, drow, :], in_=xzst[:, xsl, :]), 1 + xsl)
                for p in range(8 * row, 8 * row + 8):
                    cast_piece_do(p)
                    if p + 4 < 512:
                        cast_piece_load(p + 4)
            ev_wbf_te = C["te"]
        snap_p2 = st_snap()
        te_p2 = C["te"]
        ve_p2c = C["ve"]
        sc_p2c = C["sc"]
        MARKS["P2"] = {k: len(v) for k, v in streams.items()}

        xc_ = vf(0, KDI, TCH)
        xcf = vf(0, KDI * TCH)
        delta = vb(8192, KDI, TCH)
        deltaf = vb(8192, KDI * TCH)
        wgu_sl = [vb(12288, 2, KD, 128), vb(30640, 2, KD, 128)]
        cq_t = vf(14336, 8, TCH)
        spq_t = vf(14336, 2048)
        xpre = vb(16384, KDI, 259)
        hall = vf(16384, DS, TCH)
        hallf = vf(16384, DS * TCH)
        ct = vb(20528, DS, TCH)
        qwin = vb(22576, KD, TCH)
        zb = vb(24624, KDI, TCH)
        yg8 = vb(28720, 8, TCH)
        dxdtb = vb(29744, TCH)
        pst2 = vb(29872, TCH)
        ydtb = vb(30000, TCH)
        sgb2 = [vb(30128, TCH), vb(30512, TCH)]
        xcb = vb(32768, KDI, TCH)
        xcbf = vb(32768, KDI * TCH)
        dbx_b = vb(32768, DS, TCH)
        b_all = vb(36864, DS, TCH)
        c_all = vb(38912, DS, TCH)

        dAs_t = dAs_t + [vf(30256, TCH)]
        ev_xpre_ve = ve_p1
        ev_xpre_gp = 0
        ev_zread_ve = 0
        te_xp = 0
        t_yg8 = [None, None, None, None]
        bank_sc = [sc_p2c] * 4
        bank_ve = [ve_p2c] * 4
        unit_te = {}
        silu_sc = {}
        wgu_te = [0, 0]
        wgu_ld_tok = [None, None]
        qwin_te = [te_p2]
        t_qw = [None, None]
        pst_tok = [None, None]
        sgb_ve = [0, 0]
        exp_sc = [0] * 16

        def wgu_load(ht, cI):
            sl = ht % 2
            W("sync", "te", wgu_te[sl])
            if cI == 0 and ht < 2:
                W("sync", "ve", ve_p2c)
                W("sync", "te", te_p2)
                WT("sync", *wgu_tok[ht])
            wgu_ld_tok[sl] = LD(lambda e, ht=ht, sl=sl: e.dma_start(
                out=wgu_sl[sl][:].rearrange("p a k c -> p (a k c)"),
                in_=wgusp[:, ht, :]), 15 + sl)

        def unit_mm(u):
            ht = u % KH
            sl = ht % 2
            bank = pb[u % 4]
            WT("tensor", t_qw[1])
            WT("tensor", wgu_ld_tok[sl])
            W("tensor", "sc", bank_sc[u % 4])
            W("tensor", "ve", bank_ve[u % 4])
            for k in range(KD):
                I("tensor", lambda e, k=k, bank=bank, sl=sl: e.matmul(
                    bank[:, 0:TCH], wgu_sl[sl][:, 0, k, :], qwin[:, k, :],
                    start=(k == 0), stop=(k == KD - 1)), "te")
            for k in range(KD):
                I("tensor", lambda e, k=k, bank=bank, sl=sl: e.matmul(
                    bank[:, TCH:2 * TCH], wgu_sl[sl][:, 1, k, :], qwin[:, k, :],
                    start=(k == 0), stop=(k == KD - 1)), "te")
            unit_te[u] = C["te"]
            wgu_te[sl] = C["te"]
            qwin_te[0] = C["te"]
            if ht + 2 < KH:
                wgu_load(ht + 2, u // KH)

        def unit_silu(u):
            bank = pb[u % 4]
            W("scalar", "te", unit_te[u])
            W("scalar", "ve", sgb_ve[u % 2])
            silu_sc[u] = I("scalar", lambda e, bank=bank, sl2=u % 2: e.activation(
                sgb2[sl2][:], bank[:, 0:TCH], AF.Silu), "sc")

        def unit_pst(u):
            ht = u % KH
            ucol = (u // KH) * TCH
            sl2 = u % 2
            W("vector", "sc", silu_sc[u])
            WT("vector", pst_tok[0])
            pv = I("vector", lambda e, bank=pb[u % 4], sl2=sl2: e.tensor_tensor(
                pst2[:], sgb2[sl2][:], bank[:, TCH:2 * TCH], ALU.mult), "ve")
            bank_ve[u % 4] = pv
            bank_sc[u % 4] = silu_sc[u]
            sgb_ve[u % 2] = pv
            W("gpsimd", "ve", pv)
            pst_tok[0] = ST(lambda e, ht=ht, w4=ucol // 512, coff=ucol % 512,
                            sl2=sl2: e.dma_start(
                out=psp[:, w4, ht, coff:coff + TCH], in_=pst2[:]), 6 + sl2)

        for cI in range(NCH):
            col0 = cI * TCH
            w4 = col0 // 512
            coff = col0 % 512
            W("sync", "ve", ev_xpre_ve)
            W("sync", "gp", ev_xpre_gp)
            W("sync", "te", qwin_te[0])
            if cI == 0:
                for tok in snap_qsp:
                    WT("sync", tok)
            t_qw = [LD(lambda e, col0=col0, h2=h2: e.dma_start(
                out=qwin[:, h2 * 8:(h2 + 1) * 8, :],
                in_=qsp[:, h2 * 8:(h2 + 1) * 8, col0:col0 + TCH]), 7)
                for h2 in range(2)]
            wgu_load(0, cI)
            wgu_load(1, cI)
            if cI == 0:
                for tok in snap_p2:
                    WT("sync", tok)
                W("vector", "te", te_p2)
                W("vector", "sc", sc_p2c)
                for tok in snap_qsp:
                    WT("vector", tok)
                WT("vector", *wgu_tok[KH - 1])
                ve_z0 = I("vector", lambda e: e.memset(
                    xpre[:].rearrange("p a b -> p (a b)"), 0.0), "ve")
                W("sync", "ve", ve_z0)
                for q4 in range(4):
                    t_xpre = LD(lambda e, q4=q4: e.dma_start(
                        out=xpre[:, q4 * 8:(q4 + 1) * 8, 3:259],
                        in_=xsp[:, q4 * 8:(q4 + 1) * 8, 0:TCH]), 5)
            else:
                for q4 in range(4):
                    t_xpre = LD(lambda e, col0=col0, q4=q4: e.dma_start(
                        out=xpre[:, q4 * 8:(q4 + 1) * 8, :],
                        in_=xsp[:, q4 * 8:(q4 + 1) * 8, col0 - 3: col0 + TCH]), 5)
            W("sync", "ve", ev_zread_ve)
            for q4 in range(4):
                t_z = LD(lambda e, col0=col0, q4=q4: e.dma_start(
                    out=zb[:, q4 * 8:(q4 + 1) * 8, :],
                    in_=zsp[:, q4 * 8:(q4 + 1) * 8, col0:col0 + TCH]), 6)
            t_z_last = t_z

            WT("vector", t_xpre)
            I("vector", lambda e: e.tensor_tensor(
                xc_[:], xpre[:, :, 0:TCH],
                convw_sb[:, :, 0:1].broadcast_to([128, KDI, TCH]), ALU.mult), "ve")
            for j in (1, 2, 3):
                for hh in range(4):
                    I("vector", lambda e, j=j, hh=hh: e.tensor_tensor(
                        cq_t[:], xpre[:, hh * 8:(hh + 1) * 8, j:j + TCH],
                        convw_sb[:, hh * 8:(hh + 1) * 8, j:j + 1].broadcast_to([128, 8, TCH]),
                        ALU.mult), "ve")
                    I("vector", lambda e, hh=hh: e.tensor_tensor(
                        xc_[:, hh * 8:(hh + 1) * 8, :],
                        xc_[:, hh * 8:(hh + 1) * 8, :], cq_t[:], ALU.add), "ve")
            I("vector", lambda e: e.tensor_tensor(
                xc_[:], xc_[:],
                convb_sb[:].unsqueeze(2).broadcast_to([128, KDI, TCH]), ALU.add), "ve")
            ve_conv = C["ve"]
            W("scalar", "ve", ve_conv)
            sc_silu = I("scalar", lambda e: e.activation(xcf, xcf, AF.Silu), "sc")
            WT("scalar", t_z_last)
            W("scalar", "ve", ev_zread_ve)
            sc_zsilu = I("scalar", lambda e: e.activation(
                zb[:].rearrange("p a b -> p (a b)"),
                zb[:].rearrange("p a b -> p (a b)"), AF.Silu), "sc")
            W("vector", "sc", sc_silu)
            ve_xcb = I("vector", lambda e: e.tensor_copy(xcbf, xcf), "ve")

            p_xd = pb[4][:, 0:TCH]
            p_bc = pb[4][0:32, TCH:2 * TCH]
            W("tensor", "ve", ve_xcb)
            for k in range(KDI):
                I("tensor", lambda e, k=k: e.matmul(
                    p_xd, wx_bf[:, k, 0:128], xcb[:, k, :],
                    start=(k == 0), stop=(k == KDI - 1)), "te")
            for k in range(KDI):
                I("tensor", lambda e, k=k: e.matmul(
                    p_bc, wx_bf[:, k, 128:160], xcb[:, k, :],
                    start=(k == 0), stop=(k == KDI - 1)), "te")
            te_xp = C["te"]
            W("scalar", "te", te_xp)
            I("scalar", lambda e: e.activation(dtlo[:], p_xd, AF.Copy), "sc")
            sc_xp = I("scalar", lambda e: e.activation(bc_sb[:], p_bc, AF.Copy), "sc")

            wave_sc = sc_xp
            slots = [pb[4 + s // 2][:, (s % 2) * TCH:(s % 2 + 1) * TCH]
                     for s in range(8)]
            for w in range(8):
                W("tensor", "sc", wave_sc)
                te_w = 0
                for s in range(8):
                    j = w * 8 + s
                    if j < KDI:
                        te_w = I("tensor", lambda e, j=j, s=s: e.matmul(
                            slots[s], wdt_bf[:, j * 128:(j + 1) * 128], dtlo[:],
                            start=True, stop=True), "te")
                    else:
                        n = j - KDI
                        te_w = I("tensor", lambda e, n=n, s=s: e.matmul(
                            slots[s], oneh[:, n * 128:(n + 1) * 128], bc_sb[:],
                            start=True, stop=True), "te")
                W("scalar", "te", te_w)
                for s in range(8):
                    j = w * 8 + s
                    if j < KDI:
                        wave_sc = I("scalar", lambda e, j=j, s=s: e.activation(
                            delta[:, j, :], slots[s], AF.Identity,
                            bias=dtb_sb[:, j:j + 1]), "sc")
                    else:
                        n = j - KDI
                        dst = b_all if n < 16 else c_all
                        dn = n if n < 16 else n - 16
                        wave_sc = I("scalar", lambda e, dst=dst, dn=dn, s=s: e.activation(
                            dst[:, dn, :], slots[s], AF.Copy), "sc")

            ve_sp_add = 0
            for hh in range(4):
                dsl = deltaf.rearrange("p (a b) -> p a b", b=2048)[:, hh, :]
                W("scalar", "ve", ve_sp_add)
                I("scalar", lambda e, dsl=dsl: e.activation(spq_t[:], dsl, AF.Abs), "sc")
                I("scalar", lambda e: e.activation(spq_t[:], spq_t[:], AF.Exp, scale=-1.0), "sc")
                I("scalar", lambda e: e.activation(spq_t[:], spq_t[:], AF.Ln, bias=1.0), "sc")
                sc_sp = I("scalar", lambda e, dsl=dsl: e.activation(dsl, dsl, AF.Relu), "sc")
                W("vector", "sc", sc_sp)
                ve_sp_add = I("vector", lambda e, dsl=dsl: e.tensor_tensor(dsl, dsl, spq_t[:], ALU.add), "ve")
            ve_pre = C["ve"]

            fin_state = {"yg_ve": 0, "t_yg8": t_yg8[0]}

            def emit_dxdt_dbx(dt):
                I("vector", lambda e, dt=dt: e.tensor_tensor(
                    dxdtb[:], delta[:, dt, :], xc_[:, dt, :], ALU.mult), "ve")
                return I("vector", lambda e: e.tensor_tensor(
                    dbx_b[:], dxdtb[:].unsqueeze(1).broadcast_to([128, DS, TCH]),
                    b_all[:], ALU.mult), "ve")

            def emit_exps(dt, ve_dbx):
                for n in range(DS):
                    W("scalar", "ve", ve_dbx)
                    exp_sc[n] = I("scalar", lambda e, dt=dt, n=n: e.activation(
                        dAs_t[n][:], delta[:, dt, :], AF.Exp,
                        scale=a_sb[:, dt, n:n + 1]), "sc")

            ve_dbx = emit_dxdt_dbx(0)
            emit_exps(0, ve_dbx)
            for dt in range(KDI):
                g = dt % 8
                unit_mm(cI * KH + 2 * dt)
                unit_mm(cI * KH + 2 * dt + 1)
                for n in range(DS):
                    W("vector", "sc", exp_sc[n])
                    I("vector", lambda e, n=n, dt=dt: e.tensor_tensor_scan(
                        hall[:, n, :], dAs_t[n][:], dbx_b[:, n, :], hcar[:, dt, n:n + 1],
                        ALU.mult, ALU.add), "ve", nodep=(n > 0))
                sl_ve = C["ve"]
                I("vector", lambda e: e.tensor_tensor(ct[:], hall[:], c_all[:], ALU.mult), "ve")
                I("vector", lambda e: e.tensor_tensor(
                    ct[:, 0:8, :], ct[:, 0:8, :], ct[:, 8:16, :], ALU.add), "ve")
                I("vector", lambda e: e.tensor_tensor(
                    ct[:, 0:4, :], ct[:, 0:4, :], ct[:, 4:8, :], ALU.add), "ve")
                I("vector", lambda e: e.tensor_tensor(
                    ct[:, 0:2, :], ct[:, 0:2, :], ct[:, 2:4, :], ALU.add), "ve")
                I("vector", lambda e: e.tensor_tensor(
                    ydtb[:], ct[:, 0, :], ct[:, 1, :], ALU.add), "ve")
                I("vector", lambda e, dt=dt: e.scalar_tensor_tensor(
                    ydtb[:], xc_[:, dt, :], d_sb[:, dt:dt + 1], ydtb[:],
                    ALU.mult, ALU.add), "ve")
                if dt == 0:
                    W("vector", "sc", sc_zsilu)
                if g == 0:
                    WT("vector", fin_state["t_yg8"])
                fin_state["yg_ve"] = I("vector", lambda e, g=g, dt=dt: e.tensor_tensor(
                    yg8[:, g, :], ydtb[:], zb[:, dt, :], ALU.mult), "ve")
                if g == 7:
                    gi = dt // 8
                    ve_g = C["ve"]
                    W("gpsimd", "ve", ve_g)
                    fin_state["t_yg8"] = ST(lambda e, w4=w4, coff=coff, gi=gi: e.dma_start(
                        out=ygsp[:, w4, gi * 8:(gi + 1) * 8, coff:coff + TCH],
                        in_=yg8[:]), 5)
                if dt + 1 < KDI:
                    ve_dbx = emit_dxdt_dbx(dt + 1)
                if dt > 0:
                    unit_pst(cI * KH + 2 * dt - 2)
                    unit_pst(cI * KH + 2 * dt - 1)
                W("scalar", "ve", sl_ve)
                I("scalar", lambda e, dt=dt: e.activation(
                    hcar[:, dt, :], hall[:, :, TCH - 1], AF.Copy), "sc")
                if dt + 1 < KDI:
                    emit_exps(dt + 1, ve_dbx)
                unit_silu(cI * KH + 2 * dt)
                unit_silu(cI * KH + 2 * dt + 1)
            unit_pst(cI * KH + 62)
            unit_pst(cI * KH + 63)
            t_yg8[0] = fin_state["t_yg8"]
            ev_zread_ve = fin_state["yg_ve"]
            ev_xpre_ve = C["ve"]
            ev_xpre_gp = C["gp"]
        snap_p3 = st_snap()
        ve_p3 = C["ve"]
        sc_p3 = C["sc"]
        gp_p3 = C["gp"]

        snap_p5 = snap_p3
        ve_p5 = ve_p3
        sc_p5 = sc_p3
        MARKS["P5"] = {k: len(v) for k, v in streams.items()}

        wdb = vb(0, KH, 512)
        pwinH = [vb(16384, 32, 512), vb(24576, 32, 512)]
        wd_big = [vf(32768, 4, 512), vf(34816, 4, 512)]
        wdT_t = wdT.rearrange("(k p) l -> p k l", p=128)
        ostf = vf(36864, 2, 512)
        t_ost2 = [None, None]
        te_p6a_grp = 0
        te_p3c = C["te"]
        ev_bank6 = [sc_p3] * 4
        ev_sgn = [0, 0]
        W("tensor", "ve", ve_p3)
        W("sync", "ve", ve_p5)
        W("sync", "sc", sc_p5)
        W("sync", "te", te_p3c)
        for tok in snap_p3:
            WT("sync", tok)
        pw_tok = [LD(lambda e, h=h: e.dma_start(
            out=pwinH[h][:], in_=psp[:, 0, h * 32:(h + 1) * 32, :]), 12 + h)
            for h in range(2)]
        pw_te = [0, 0]
        for mg in range(4):
            for q in range(16):
                sl = q % 2
                if mg == 0 and q == 0:
                    W("sync", "ve", ve_p5)
                    W("sync", "sc", sc_p5)
                    for tok in snap_p3:
                        WT("sync", tok)
                W("sync", "sc", ev_sgn[sl])
                t_wd = LD(lambda e, q=q, mg=mg, sl=sl: e.dma_start(
                    out=wd_big[sl][:],
                    in_=wdT_t[:, q * 4:(q + 1) * 4, mg * 512:(mg + 1) * 512]), 10 + sl)
                WT("scalar", t_wd)
                if mg == 0 and q == 0:
                    W("scalar", "ve", ve_p5)
                if q == 0:
                    W("scalar", "te", te_p6a_grp)
                ev_sgn[sl] = I("scalar", lambda e, q=q, sl=sl: e.activation(
                    wdb[:, q * 4:(q + 1) * 4, :].rearrange("p a b -> p (a b)"),
                    wd_big[sl][:].rearrange("p a b -> p (a b)"), AF.Sign), "sc")
            sc_wd = C["sc"]
            for wdw in range(4):
                widx = mg * 4 + wdw
                nw = (widx + 1) % 4
                W("tensor", "sc", sc_wd)
                WT("tensor", pw_tok[0])
                for ms in range(4):
                    bank = (ms + wdw) % 4
                    W("tensor", "sc", ev_bank6[bank])
                    for k in range(32):
                        I("tensor", lambda e, k=k, ms=ms, bank=bank: e.matmul(
                            pb[bank][:], wdb[:, k, ms * 128:(ms + 1) * 128],
                            pwinH[0][:, k, :],
                            start=(k == 0), stop=False), "te")
                pw_te[0] = C["te"]
                if widx + 1 < 16:
                    W("sync", "te", pw_te[0])
                    pw_tok[0] = LD(lambda e, nw=nw: e.dma_start(
                        out=pwinH[0][:], in_=psp[:, nw, 0:32, :]), 12)
                WT("tensor", pw_tok[1])
                for ms in range(4):
                    bank = (ms + wdw) % 4
                    for k in range(32):
                        I("tensor", lambda e, k=k, ms=ms, bank=bank: e.matmul(
                            pb[bank][:], wdb[:, 32 + k, ms * 128:(ms + 1) * 128],
                            pwinH[1][:, k, :],
                            start=False, stop=(k == 31)), "te")
                    te_mm = C["te"]
                    W("scalar", "te", te_mm)
                    osl = (ms + 4 * wdw) % 2
                    WT("scalar", t_ost2[osl])
                    sc_ost = I("scalar", lambda e, bank=bank, osl=osl: e.activation(
                        ostf[:, osl, :], pb[bank][:], AF.Copy), "sc")
                    ev_bank6[bank] = sc_ost
                    W("gpsimd", "sc", sc_ost)
                    t_ost2[osl] = ST(lambda e, mg=mg, ms=ms, w=wdw, osl=osl: e.dma_start(
                        out=offn[:, mg * 4 + ms, w * 512:(w + 1) * 512], in_=ostf[:, osl, :]), 1 + 3 * osl)
                pw_te[1] = C["te"]
                if widx + 1 < 16:
                    W("sync", "te", pw_te[1])
                    pw_tok[1] = LD(lambda e, nw=nw: e.dma_start(
                        out=pwinH[1][:], in_=psp[:, nw, 32:64, :]), 13)
                te_p6a_grp = C["te"]
        snap_p6a = st_snap()
        sc_p6a = C["sc"]
        te_p6a = C["te"]

        wob = vb(0, KDI, 512)
        ygwinH = [vb(8192, 16, 512), vb(12288, 16, 512)]
        wo_big = [vf(16384, 8, 512), vf(20480, 8, 512)]
        xct2 = [vf(24576, 512), vf(25088, 512)]
        oft2 = [vf(25600, 512), vf(26112, 512)]
        ostg2 = [vf(26624, 512), vf(27136, 512)]
        woT_t = woT.rearrange("(k p) l -> p k l", p=128)
        xcol_t = xcol.rearrange("(t p) l -> p t l", p=128)
        ev_bank_cmb = [0, 0, 0, 0]
        ev_cmb_slot = [0, 0]
        t_out2 = [None, None]
        t_xct2 = [None, None]
        t_oft2 = [None, None]
        te_p6b_grp = 0
        ev_cst = [0, 0]
        for mg in range(4):
            for q in range(4):
                sl = q % 2
                if mg == 0 and q == 0:
                    W("sync", "sc", sc_p6a)
                W("sync", "sc", ev_cst[sl])
                t_wo = LD(lambda e, q=q, mg=mg, sl=sl: e.dma_start(
                    out=wo_big[sl][:],
                    in_=woT_t[:, q * 8:(q + 1) * 8, mg * 512:(mg + 1) * 512]), 10 + sl)
                WT("scalar", t_wo)
                if mg == 0 and q == 0:
                    W("scalar", "sc", sc_p6a)
                if q == 0:
                    W("scalar", "te", te_p6b_grp)
                ev_cst[sl] = I("scalar", lambda e, q=q, sl=sl: e.activation(
                    wob[:, q * 8:(q + 1) * 8, :].rearrange("p a b -> p (a b)"),
                    wo_big[sl][:].rearrange("p a b -> p (a b)"), AF.Copy), "sc")
            sc_wo = C["sc"]
            for wdw in range(4):
                if mg == 0 and wdw == 0:
                    for tok in snap_p3:
                        WT("sync", tok)
                    W("sync", "sc", sc_p6a)
                if mg == 0 and wdw == 0:
                    W("sync", "te", te_p6a)
                    yg_tok = [LD(lambda e, h=h: e.dma_start(
                        out=ygwinH[h][:], in_=ygsp[:, 0, h * 16:(h + 1) * 16, :]),
                        12 + h) for h in range(2)]
                    yg_te = [0, 0]
                nw6 = (mg * 4 + wdw + 1) % 4
                W("tensor", "sc", sc_wo)
                WT("tensor", yg_tok[0])
                for ms in range(4):
                    bank = (ms + wdw) % 4
                    W("tensor", "ve", ev_bank_cmb[bank])
                    for k in range(16):
                        I("tensor", lambda e, k=k, ms=ms, bank=bank: e.matmul(
                            pb[bank][:], wob[:, k, ms * 128:(ms + 1) * 128],
                            ygwinH[0][:, k, :],
                            start=(k == 0), stop=False), "te")
                yg_te[0] = C["te"]
                if mg * 4 + wdw + 1 < 16:
                    W("sync", "te", yg_te[0])
                    yg_tok[0] = LD(lambda e, nw6=nw6: e.dma_start(
                        out=ygwinH[0][:], in_=ygsp[:, nw6, 0:16, :]), 12)
                WT("tensor", yg_tok[1])
                for ms in range(4):
                    bank = (ms + wdw) % 4
                    gidx = ms + 4 * wdw
                    osl = gidx % 2
                    row = mg * 4 + ms
                    for k in range(16):
                        I("tensor", lambda e, k=k, ms=ms, bank=bank: e.matmul(
                            pb[bank][:], wob[:, 16 + k, ms * 128:(ms + 1) * 128],
                            ygwinH[1][:, k, :],
                            start=False, stop=(k == 15)), "te")
                    te_mm = C["te"]
                    W("sync", "ve", ev_cmb_slot[osl])
                    if mg == 0 and wdw == 0 and ms == 0:
                        for tok in snap_p6a:
                            WT("sync", tok)
                    t_xct2[osl] = LD(lambda e, row=row, w=wdw, osl=osl: e.dma_start(
                        out=xct2[osl], in_=xcol_t[:, row, w * 512:(w + 1) * 512]), 5 + 2 * osl)
                    t_oft2[osl] = LD(lambda e, row=row, w=wdw, osl=osl: e.dma_start(
                        out=oft2[osl], in_=offn[:, row, w * 512:(w + 1) * 512]), 6 + 8 * osl)
                    W("vector", "te", te_mm)
                    WT("vector", t_xct2[osl], t_oft2[osl])
                    WT("vector", t_out2[osl])
                    I("vector", lambda e, bank=bank, osl=osl: e.tensor_tensor(
                        ostg2[osl], pb[bank][:], xct2[osl], ALU.add), "ve")
                    ev_cmb_slot[osl] = I("vector", lambda e, osl=osl: e.tensor_tensor(
                        ostg2[osl], ostg2[osl], oft2[osl], ALU.add), "ve")
                    ev_bank_cmb[bank] = ev_cmb_slot[osl]
                    W("gpsimd", "ve", ev_cmb_slot[osl])
                    t_out2[osl] = ST(lambda e, row=row, w=wdw, osl=osl: e.dma_start(
                        out=outT[:, row, w * 512:(w + 1) * 512], in_=ostg2[osl]), 2 + osl)
                yg_te[1] = C["te"]
                if mg * 4 + wdw + 1 < 16:
                    W("sync", "te", yg_te[1])
                    yg_tok[1] = LD(lambda e, nw6=nw6: e.dma_start(
                        out=ygwinH[1][:], in_=ygsp[:, nw6, 16:32, :]), 13)
                te_p6b_grp = C["te"]
        for tok in st_snap():
            WT("gpsimd", tok)
        if CUT is not None:
            cut = MARKS[CUT]
            for k in streams:
                streams[k] = streams[k][:cut[k]]

        @block.sync
        def _(eng):
            for f in streams["sync"]:
                f(eng)

        @block.tensor
        def _(eng):
            for f in streams["tensor"]:
                f(eng)

        @block.vector
        def _(eng):
            for f in streams["vector"]:
                f(eng)

        @block.scalar
        def _(eng):
            for f in streams["scalar"]:
                f(eng)

        @block.gpsimd
        def _(eng):
            for f in streams["gpsimd"]:
                f(eng)

    return nc


def kernel(L0_attn_hidden_states, L0_attn_in_proj_w, L0_attn_conv_w,
           L0_attn_conv_b, L0_attn_x_proj_w, L0_attn_dt_proj_w,
           L0_attn_dt_proj_b, L0_attn_A_log, L0_attn_D, L0_attn_out_proj_w,
           L0_ffn_W_g, L0_ffn_W_u, L0_ffn_W_d):
    x = np.ascontiguousarray(np.asarray(L0_attn_hidden_states, dtype=np.float32))
    B = x.shape[0]

    def t(a):
        return np.ascontiguousarray(np.asarray(a, dtype=np.float32).T)

    def perpart(a):
        a = np.asarray(a, dtype=np.float32)
        sh = a.shape[1:]
        return np.ascontiguousarray(
            a.reshape(KDI, 128, *sh).transpose(1, 0, *range(2, 2 + len(sh))))

    winT = t(L0_attn_in_proj_w)
    wxT = np.ascontiguousarray(
        np.asarray(L0_attn_x_proj_w, np.float32).T.reshape(KDI, 128, 160).transpose(1, 0, 2))
    wdtT = t(L0_attn_dt_proj_w)
    woT = t(L0_attn_out_proj_w)
    wgT = t(L0_ffn_W_g)
    wuT = t(L0_ffn_W_u)
    wdT = t(L0_ffn_W_d)
    convw_h = perpart(L0_attn_conv_w)
    convb_h = perpart(L0_attn_conv_b)
    dtb_h = perpart(L0_attn_dt_proj_b)
    alog_h = perpart(L0_attn_A_log)
    dvec_h = perpart(L0_attn_D)

    if "nc" not in _CACHE:
        _CACHE["nc"] = build_program()
    nc = _CACHE["nc"]

    shared = dict(winT=winT, wxT=wxT, wdtT=wdtT, woT=woT, wgT=wgT, wuT=wuT,
                  wdT=wdT, convw=convw_h, convb=convb_h, dtb=dtb_h,
                  alog=alog_h, dvec=dvec_h)
    in_maps = []
    for b in range(B):
        m = dict(shared)
        m["xrow"] = np.ascontiguousarray(x[b])
        m["xcol"] = np.ascontiguousarray(x[b].T)
        in_maps.append(m)

    res = run_bass_kernel_spmd(nc, in_maps, core_ids=list(range(B)))
    outs = []
    for b in range(B):
        arr = np.asarray(res.results[b]["outT"])
        outs.append(arr.transpose(2, 1, 0).reshape(L, D))
    return np.stack(outs).astype(np.float32)
